# version 02
# baseline (speedup 1.0000x reference)
import math
import numpy as np
import ml_dtypes

import concourse.bass as bass
import concourse.mybir as mybir
from concourse.bass_utils import run_bass_kernel_spmd
from contextlib import ExitStack

F32 = mybir.dt.float32
BF16 = mybir.dt.bfloat16
I32 = mybir.dt.int32
I8 = mybir.dt.int8
AF = mybir.ActivationFunctionType
ALU = mybir.AluOpType
AX = mybir.AxisListType

NC = 8
S = 8192
H = 4096
HD = 128
ROWS = S // NC
FC = H // NC
SCALE = HD ** -0.5
M_STAB = 12.0
NEG = -1.0e9

L0_SLOTS = [(64, 768), (256, 2560), (512, None), (512, None)]
L1_SLOTS = [(512, None)] * 4
_ATTN_TRUNC = None
_PROJ_TRUNC = None
_FUSE_PHASES = 8
_OUT_TRUNC = None
_NO_CLEAR = False

BF = ml_dtypes.bfloat16


def _alibi_slopes(n):
    cp2 = 2 ** math.floor(math.log2(n))
    base = 2 ** (-(2 ** -(math.log2(cp2) - 3)))
    slopes = np.power(base, np.arange(1, 1 + cp2, dtype=np.float64))
    if cp2 != n:
        eb = 2 ** (-(2 ** -(math.log2(2 * cp2) - 3)))
        rem = min(cp2, n - cp2)
        slopes = np.concatenate(
            [slopes, np.power(eb, np.arange(1, 1 + 2 * rem, 2, dtype=np.float64))])
    return slopes.astype(np.float32)


def _sems(nc, st, names):
    return {n: st.enter_context(nc.semaphore(n)) for n in names}


class _Track:
    def __init__(self):
        self.base = {}
        self.cnt = {}

    def add(self, sem, val):
        self.cnt[sem] = self.cnt.get(sem, 0) + val


class _InsW:
    __slots__ = ("i", "tr")

    def __init__(self, i, tr):
        self.i = i
        self.tr = tr

    def then_inc(self, sem, val):
        self.tr.add(sem, val)
        self.i.then_inc(sem, val)
        return self

    def __getattr__(self, n):
        return getattr(self.i, n)


class _EngW:
    def __init__(self, e, tr):
        self.e = e
        self.tr = tr

    def wait_ge(self, sem, val, **kw):
        return self.e.wait_ge(sem, val + self.tr.base.get(sem, 0), **kw)

    def sem_inc(self, sem, val=1):
        self.tr.add(sem, val)
        return self.e.sem_inc(sem, val)

    def __getattr__(self, n):
        a = getattr(self.e, n)
        if not callable(a):
            return a
        tr = self.tr

        def w(*args, **kw):
            r = a(*args, **kw)
            return _InsW(r, tr) if hasattr(r, "then_inc") else r
        return w


class _BlkW:
    def __init__(self, b, tr):
        self.b = b
        self.tr = tr

    def _deco(self, name):
        dec = getattr(self.b, name)
        tr = self.tr

        def deco(fn):
            return dec(lambda e: fn(_EngW(e, tr)))
        return deco

    @property
    def sync(self):
        return self._deco("sync")

    @property
    def vector(self):
        return self._deco("vector")

    @property
    def scalar(self):
        return self._deco("scalar")

    @property
    def tensor(self):
        return self._deco("tensor")

    @property
    def gpsimd(self):
        return self._deco("gpsimd")



def emit_proj(nc, block, st0, s, layer, pfx, io):
    xnT = io["xnT"]
    ident_in = io["ident"]
    if layer == 0:
        WB = 1536
        w_all = io["w_all"]
        b_row = io["b_row"]
        kT_rows, nkv = 512, 4
    else:
        WB = 1296
        codesT = io["codesT_all"]
        scalesT = io["scalesT_all"]
        wffnT = io["wffnT"]
        bffn_row = io["bffn_row"]
        loraA_arr = io["loraA_arr"]
        loraB_arr = io["loraB_arr"]
        lsi = io["lsi"]
        cos_in = io["cos_arr"]
        sin_in = io["sin_arr"]
        kT_rows, nkv = 128, 1
    qT_o = io["qT"]
    kT_o = io["kT"]
    v_o = io["v4"]
    if layer == 0:
        xred_o = io["xredT"]
    else:
        ffn_o = io["ffn1"]

    NKT = 4 if layer == 0 else 1
    if layer == 0:
        n_stores = (4 + NKT) + 16 + 4
    else:
        n_stores = (4 + NKT) + 4 + 1
    CH = 16 if _PROJ_TRUNC is None else _PROJ_TRUNC

    with ExitStack() as st:
        sb = lambda n, s_, d: st.enter_context(nc.sbuf_tensor(pfx + n, s_, d))
        psum = lambda n, s_, d: st.enter_context(nc.psum_tensor(pfx + n, s_, d))
        ct_p = [s["ct0"], s["ct1"]]
        xnld_p = [s["xnld0"], s["xnld1"]]
        VCH = [0]
        def mkedge(v):
            def edge(ins):
                VCH[0] += 1
                ins.then_inc(s["vch"], 1)
                v.wait_ge(s["vch"], VCH[0])
                return ins
            return edge
        W_sb = sb("W_sb", [128, 32 * WB], BF16)
        ident = sb("ident_sb", [128, 128], BF16)
        ones_sb = sb("ones_sb", [1, 128], BF16)
        dummyp = sb("dummyp", [1, 8], F32)
        brow_sb = sb("brow_sb", [1, WB if layer == 0 else FC], BF16)
        if layer == 1:
            sc_sb = sb("sc_sb", [128, 6 * 32], F32)
            beff_sb = sb("beff_sb", [16, FC], BF16)
            selbc = sb("selbc", [128, 3], F32)
        psA = [psum(f"psA{i}", [128, 512], F32) for i in range(2)]
        psB = [psum(f"psB{i}", [128, 512], F32) for i in range(2)]
        psC = [psum(f"psC{i}", [128, 512], F32) for i in range(2)]
        tp = [psum(f"tp{i}", [128, 1024], BF16) for i in range(2)]

        if layer == 0:
            @block.gpsimd
            def _(g):
                g.dma_start(ident[:], ident_in).then_inc(s["ident"], 16)
                g.dma_start(W_sb[:].rearrange("p (n f) -> p n f", f=WB),
                            w_all.rearrange("(n p) f -> p n f", p=128)
                            ).then_inc(s["wl"], 16)
                g.dma_start(brow_sb[:], b_row).then_inc(s["wl"], 16)
            WL_FINAL = 32

            @block.vector
            def _(v):
                v.memset(ones_sb[:], 1.0).then_inc(s["sel"], 1)
            SEL_FINAL = 1
            TPC_FINAL = 0
        else:
            with ExitStack() as stp:
                sbp = lambda n, s_, d: stp.enter_context(nc.sbuf_tensor(pfx + n, s_, d))
                ct_sb = [sbp(f"ct{i}", [128, H], I32) for i in range(2)]
                deq_sb = [sbp(f"dq{i}", [128, H], BF16) for i in range(2)]
                a_sb = sbp("a_sb", [128, 32 * 48], F32)
                b3_sb = sbp("b3_sb", [16, 3 * FC], F32)
                lsi_sb = sbp("lsi_sb", [1, 4], I32)
                lsif = sbp("lsif", [1, 4], F32)
                ge = sbp("ge", [1, 4], F32)
                tsc = sbp("tsc", [1, 8], F32)
                sel_sb = sbp("sel_sb", [1, 3], BF16)
                atmp = sbp("atmp", [128, 16], F32)
                atmp2 = sbp("atmp2", [128, 16], F32)

                @block.gpsimd
                def _(g):
                    g.dma_start(ident[:], ident_in).then_inc(s["ident"], 16)
                    g.dma_start(W_sb[:].rearrange("p (n f) -> p n f", f=WB)
                                [:, :, 784:1296],
                                wffnT.rearrange("(n p) f -> p n f", p=128)
                                ).then_inc(s["wl"], 16)
                    g.dma_start(brow_sb[:], bffn_row).then_inc(s["wl"], 16)
                    g.dma_start(sc_sb[:].rearrange("p (n f) -> p n f", f=32),
                                scalesT.rearrange("(n p) f -> p n f", p=128)
                                ).then_inc(s["wl"], 16)
                    g.dma_start(a_sb[:], loraA_arr).then_inc(s["wl"], 16)
                    g.dma_start(b3_sb[:], loraB_arr).then_inc(s["wl"], 16)
                    g.dma_start(lsi_sb[:], lsi).then_inc(s["wl"], 16)
                    for ft in range(6):
                        if ft >= 2:
                            g.wait_ge(s["prep"], ft - 1)
                        g.dma_start(ct_sb[ft % 2][:], codesT[ft * 128:(ft + 1) * 128, :]
                                    ).then_inc(ct_p[ft % 2], 16)
                WL_FINAL = 16 * 6

                @block.scalar
                def _(sc):
                    for ft in range(6):
                        sc.wait_ge(ct_p[ft % 2], 16 * (ft // 2 + 1))
                        if ft >= 2:
                            sc.wait_ge(s["tpw"], 32 * (ft - 1))
                        sc.activation(deq_sb[ft % 2][:], ct_sb[ft % 2][:], AF.Copy,
                                      bias=-8.0).then_inc(s["prep"], 1)

                @block.vector
                def _(v):
                    edge = mkedge(v)
                    v.wait_ge(s["wl"], WL_FINAL)
                    edge(v.tensor_copy(lsif[:], lsi_sb[:]))
                    edge(v.tensor_single_scalar(ge[:], lsif[:], 0.0, ALU.is_le))
                    edge(v.tensor_scalar(tsc[:, 0:1], ge[:, 3:4], 3.0, None, ALU.add))
                    edge(v.tensor_tensor(tsc[:, 1:2], ge[:, 2:3], tsc[:, 0:1], ALU.mult))
                    edge(v.tensor_scalar(tsc[:, 2:3], ge[:, 1:2], 2.0, None, ALU.mult))
                    edge(v.tensor_scalar(tsc[:, 3:4], ge[:, 1:2], -1.0, 1.0, ALU.mult,
                                         ALU.add))
                    edge(v.tensor_tensor(tsc[:, 4:5], tsc[:, 3:4], ge[:, 0:1], ALU.mult))
                    edge(v.tensor_tensor(tsc[:, 5:6], tsc[:, 2:3], tsc[:, 4:5], ALU.add))
                    edge(v.tensor_scalar(tsc[:, 6:7], ge[:, 2:3], -1.0, 1.0, ALU.mult,
                                         ALU.add))
                    edge(v.tensor_tensor(tsc[:, 7:8], tsc[:, 6:7], tsc[:, 5:6], ALU.mult))
                    edge(v.tensor_tensor(tsc[:, 0:1], tsc[:, 1:2], tsc[:, 7:8], ALU.add))
                    for j in range(3):
                        edge(v.tensor_single_scalar(sel_sb[:, j:j + 1], tsc[:, 0:1],
                                                    float(j + 2), ALU.is_equal))
                    v.memset(ones_sb[:], 1.0).then_inc(s["sel"], 1)

                @block.tensor
                def _(t):
                    t.wait_ge(s["sel"], 1)
                    t.matmul(psA[0][:, 0:3], ones_sb[:], sel_sb[:],
                             start=True, stop=True).then_inc(s["sel"], 1)

                @block.vector
                def _(v):
                    edge = mkedge(v)
                    v.wait_ge(s["sel"], 2)
                    edge(v.tensor_copy(selbc[:], psA[0][:, 0:3]))
                    v.memset(dummyp[0:1, :], 0.0).then_inc(s["sel"], 1)
                    edge(v.tensor_scalar(beff_sb[0:16, :], b3_sb[0:16, 0:FC],
                                         selbc[0:16, 0:1], None, ALU.mult))
                    edge(v.scalar_tensor_tensor(beff_sb[0:16, :], b3_sb[0:16, FC:2 * FC],
                                                selbc[0:16, 1:2], beff_sb[0:16, :],
                                                ALU.mult, ALU.add))
                    v.scalar_tensor_tensor(beff_sb[0:16, :], b3_sb[0:16, 2 * FC:3 * FC],
                                           selbc[0:16, 2:3], beff_sb[0:16, :],
                                           ALU.mult, ALU.add).then_inc(s["sel"], 1)
                    for ht in range(32):
                        edge(v.tensor_scalar(atmp[:], a_sb[:, ht * 48:ht * 48 + 16],
                                             selbc[:, 0:1], None, ALU.mult))
                        edge(v.scalar_tensor_tensor(atmp2[:],
                                                    a_sb[:, ht * 48 + 16:ht * 48 + 32],
                                                    selbc[:, 1:2], atmp[:],
                                                    ALU.mult, ALU.add))
                        edge(v.scalar_tensor_tensor(
                            W_sb[:, ht * WB + 768:ht * WB + 784],
                            a_sb[:, ht * 48 + 32:ht * 48 + 48], selbc[:, 2:3], atmp2[:],
                            ALU.mult, ALU.add))
                    v.memset(dummyp[0:1, :], 0.0).then_inc(s["sel"], 1)
                    nt = 0
                    for ft in range(6):
                        v.wait_ge(s["prep"], ft + 1)
                        for ht in range(32):
                            ins = v.tensor_scalar(
                                deq_sb[ft % 2][:, ht * 128:(ht + 1) * 128],
                                deq_sb[ft % 2][:, ht * 128:(ht + 1) * 128],
                                sc_sb[:, ft * 32 + ht:ft * 32 + ht + 1], None, ALU.mult)
                        ins.then_inc(s["dq"], 1)
                        dst0 = [0, 128, 256, 384, 512, 640][ft]
                        for ht in range(32):
                            v.wait_ge(s["tpw"], nt + 1)
                            v.tensor_copy(W_sb[:, ht * WB + dst0:ht * WB + dst0 + 128],
                                          tp[nt % 2][:, 0:128]).then_inc(s["tpc"], 1)
                            nt += 1

                @block.tensor
                def _(t):
                    t.wait_ge(s["ident"], 16)
                    nt = 0
                    for ft in range(6):
                        t.wait_ge(s["dq"], ft + 1)
                        for ht in range(32):
                            if nt >= 2:
                                t.wait_ge(s["tpc"], nt - 1)
                            t.transpose(tp[nt % 2][:, 0:128],
                                        deq_sb[ft % 2][:, ht * 128:(ht + 1) * 128],
                                        ident[:]).then_inc(s["tpw"], 1)
                            nt += 1
            SEL_FINAL = 5
            TPC_FINAL = 192

        xn_sb = [sb(f"xn{i}", [128, 32 * 512], BF16) for i in range(2)]
        qnat = [sb(f"qnat{i}", [128, 512], BF16) for i in range(2)]
        knat = [sb(f"knat{i}", [128, 128 * NKT], BF16) for i in range(2)]
        qkstage = sb("qks", [128, (4 + NKT) * 512], BF16)
        vstage = sb("vs", [128, 4 * (512 if layer == 0 else 128)], BF16)
        if layer == 0:
            xred = sb("xr", [128, 4 * 512], BF16)
            xrt = [sb(f"xrt{i}", [128, 512], BF16) for i in range(4)]
        else:
            cosc = [sb(f"cosc{i}", [128, 256], F32) for i in range(2)]
            sinc = [sb(f"sinc{i}", [128, 256], F32) for i in range(2)]
            qf32 = [sb(f"qf{i}", [128, 512], F32) for i in range(2)]
            kf32 = [sb(f"kf{i}", [128, 128], F32) for i in range(2)]
            rtmp1 = sb("rtmp1", [128, 64], F32)
            rtmp2 = sb("rtmp2", [128, 64], F32)
            ffnstage = sb("fs", [128, 4 * 512], BF16)
            t1_sb = [sb(f"t1_{i}", [128, 16], BF16) for i in range(2)]
            t1T_sb = [sb(f"t1T_{i}", [16, 128], BF16) for i in range(2)]

        def prep_gates(eng):
            eng.wait_ge(s["wl"], WL_FINAL)
            eng.wait_ge(s["ident"], 16)
            eng.wait_ge(s["sel"], SEL_FINAL)
            if layer == 1:
                eng.wait_ge(s["tpc"], TPC_FINAL)

        N_LD = 1 if layer == 0 else 3

        @block.sync
        def _(sync):
            prep_gates(sync)
            for cs in range(CH):
                if cs >= 2:
                    sync.wait_ge(s["chunkdone"], cs - 1)
                sync.dma_start(
                    xn_sb[cs % 2][:].rearrange("p (n f) -> p n f", f=512),
                    xnT[:, cs * 512:(cs + 1) * 512].rearrange("(n p) f -> p n f", p=128),
                ).then_inc(xnld_p[cs % 2], 16)
                if layer == 1:
                    sync.dma_start(cosc[cs % 2][:], cos_in[:, cs * 256:(cs + 1) * 256]
                                   ).then_inc(xnld_p[cs % 2], 16)
                    sync.dma_start(sinc[cs % 2][:], sin_in[:, cs * 256:(cs + 1) * 256]
                                   ).then_inc(xnld_p[cs % 2], 16)

        def emit_transposes(t, j):
            t.wait_ge(s["qn"], j + 1)
            if j >= 2:
                t.wait_ge(s["tpev"], j - 1)
            par = j % 2
            for ft in range(4):
                ins = t.transpose(tp[par][:, ft * 128:(ft + 1) * 128],
                                  qnat[par][:, ft * 128:(ft + 1) * 128], ident[:])
            for kt in range(NKT):
                ins = t.transpose(tp[par][:, (4 + kt) * 128:(5 + kt) * 128],
                                  knat[par][:, kt * 128:(kt + 1) * 128], ident[:])
            ins.then_inc(s["tp"], 1)

        @block.tensor
        def _(t):
            prep_gates(t)
            for cs in range(CH):
                for stx in range(4):
                    i = 4 * cs + stx
                    par = i % 2
                    if stx == 0:
                        t.wait_ge(xnld_p[cs % 2], N_LD * 16 * (cs // 2 + 1))
                    if i >= 2:
                        t.wait_ge(s["evt"], i - 1)
                    lhs = lambda ht: xn_sb[cs % 2][:, ht * 512 + stx * 128:
                                                   ht * 512 + stx * 128 + 128]
                    if layer == 0:
                        for ht in range(32):
                            t.matmul(psA[par][:], lhs(ht), W_sb[:, ht * WB:ht * WB + 512],
                                     start=(ht == 0), stop=False)
                            t.matmul(psB[par][:], lhs(ht),
                                     W_sb[:, ht * WB + 512:ht * WB + 1024],
                                     start=(ht == 0), stop=False)
                            t.matmul(psC[par][:], lhs(ht),
                                     W_sb[:, ht * WB + 1024:ht * WB + 1536],
                                     start=(ht == 0), stop=False)
                        t.matmul(psA[par][:], ones_sb[:], brow_sb[0:1, 0:512],
                                 start=False, stop=True)
                        t.matmul(psB[par][:], ones_sb[:], brow_sb[0:1, 512:1024],
                                 start=False, stop=True)
                        t.matmul(psC[par][:], ones_sb[:], brow_sb[0:1, 1024:1536],
                                 start=False, stop=True).then_inc(s["mm"], 1)
                    else:
                        for ht in range(32):
                            t.matmul(psA[par][:], lhs(ht), W_sb[:, ht * WB:ht * WB + 512],
                                     start=(ht == 0), stop=(ht == 31))
                            t.matmul(psB[par][:, 0:272], lhs(ht),
                                     W_sb[:, ht * WB + 512:ht * WB + 784],
                                     start=(ht == 0), stop=(ht == 31))
                            t.matmul(psC[par][:], lhs(ht),
                                     W_sb[:, ht * WB + 784:ht * WB + 1296],
                                     start=(ht == 0), stop=False)
                        t.matmul(psC[par][:], ones_sb[:], brow_sb[0:1, :],
                                 start=False, stop=False).then_inc(s["mm1"], 1)
                        t.wait_ge(s["t1ev"], i + 1)
                        if i >= 2:
                            t.wait_ge(s["tpev"], i - 1)
                        t.transpose(tp[par][0:16, 0:128], t1_sb[par][:],
                                    ident[:]).then_inc(s["tt"], 1)
                        t.wait_ge(s["t1T"], i + 1)
                        t.matmul(psC[par][:], t1T_sb[par][0:16, :], beff_sb[0:16, :],
                                 start=False, stop=True).then_inc(s["mm"], 1)
                    if i >= 1:
                        emit_transposes(t, i - 1)
            emit_transposes(t, CH * 4 - 1)

        @block.vector
        def _(v):
            edge = mkedge(v)
            for cs in range(CH):
                for stx in range(4):
                    i = 4 * cs + stx
                    par = i % 2
                    if layer == 1:
                        v.wait_ge(s["mm1"], i + 1)
                        v.tensor_copy(t1_sb[par][:], psB[par][:, 256:272]
                                      ).then_inc(s["t1ev"], 1)
                        v.wait_ge(s["tt"], i + 1)
                        v.tensor_copy(t1T_sb[par][0:16, :], tp[par][0:16, 0:128]
                                      ).then_inc(s["t1T"], 1)
                    if i >= 1:
                        j = i - 1
                        v.wait_ge(s["tp"], j + 1)
                        for ft in range(4 + NKT):
                            ins = v.tensor_copy(
                                qkstage[:, ft * 512 + (j % 4) * 128:
                                        ft * 512 + (j % 4) * 128 + 128],
                                tp[j % 2][:, ft * 128:(ft + 1) * 128])
                        ins.then_inc(s["tpev"], 1)
                    v.wait_ge(s["mm"], i + 1)
                    if stx == 0 and cs >= 1:
                        v.wait_ge(s["store"], n_stores * 16 * cs)
                    if layer == 0:
                        v.tensor_copy(qnat[par][:], psA[par][:])
                        v.tensor_copy(knat[par][:], psB[par][:]).then_inc(s["qn"], 1)
                        v.tensor_copy(vstage[:, stx * 512:(stx + 1) * 512], psC[par][:]
                                      ).then_inc(s["evt"], 1)
                    else:
                        if stx == 0:
                            v.wait_ge(xnld_p[cs % 2], N_LD * 16 * (cs // 2 + 1))
                        edge(v.tensor_copy(qf32[par][:], psA[par][:]))
                        edge(v.tensor_copy(kf32[par][:], psB[par][:, 0:128]))
                        v.tensor_copy(vstage[:, stx * 128:(stx + 1) * 128],
                                      psB[par][:, 128:256])
                        v.tensor_copy(ffnstage[:, stx * 512:(stx + 1) * 512], psC[par][:]
                                      ).then_inc(s["evt"], 1)
                        cosA = cosc[cs % 2][:, stx * 64:(stx + 1) * 64]
                        sinA = sinc[cs % 2][:, stx * 64:(stx + 1) * 64]
                        for hh in range(4):
                            qe = qf32[par][:, hh * 128:hh * 128 + 64]
                            qo = qf32[par][:, hh * 128 + 64:hh * 128 + 128]
                            v.tensor_tensor(rtmp1[:], qe, cosA, ALU.mult)
                            edge(v.tensor_tensor(rtmp2[:], qo, sinA, ALU.mult))
                            edge(v.tensor_tensor(qnat[par][:, hh * 128:hh * 128 + 64],
                                                 rtmp1[:], rtmp2[:], ALU.subtract))
                            v.tensor_tensor(rtmp1[:], qo, cosA, ALU.mult)
                            edge(v.tensor_tensor(rtmp2[:], qe, sinA, ALU.mult))
                            edge(v.tensor_tensor(qnat[par][:, hh * 128 + 64:
                                                           hh * 128 + 128],
                                                 rtmp1[:], rtmp2[:], ALU.add))
                        ke = kf32[par][:, 0:64]
                        ko = kf32[par][:, 64:128]
                        v.tensor_tensor(rtmp1[:], ke, cosA, ALU.mult)
                        edge(v.tensor_tensor(rtmp2[:], ko, sinA, ALU.mult))
                        edge(v.tensor_tensor(knat[par][:, 0:64], rtmp1[:], rtmp2[:],
                                             ALU.subtract))
                        v.tensor_tensor(rtmp1[:], ko, cosA, ALU.mult)
                        edge(v.tensor_tensor(rtmp2[:], ke, sinA, ALU.mult))
                        v.tensor_tensor(knat[par][:, 64:128], rtmp1[:], rtmp2[:],
                                        ALU.add).then_inc(s["qn"], 1)
                if layer == 0:
                    for rt in range(4):
                        blocks = [xn_sb[cs % 2][:, (4 * g + rt) * 512:
                                                (4 * g + rt + 1) * 512] for g in range(8)]
                        v.tensor_tensor(xrt[0][:], blocks[0], blocks[1], ALU.add)
                        v.tensor_tensor(xrt[1][:], blocks[2], blocks[3], ALU.add)
                        v.tensor_tensor(xrt[2][:], blocks[4], blocks[5], ALU.add)
                        edge(v.tensor_tensor(xrt[3][:], blocks[6], blocks[7], ALU.add))
                        v.tensor_tensor(xrt[0][:], xrt[0][:], xrt[1][:], ALU.add)
                        edge(v.tensor_tensor(xrt[2][:], xrt[2][:], xrt[3][:], ALU.add))
                        edge(v.tensor_tensor(xred[:, rt * 512:(rt + 1) * 512],
                                             xrt[0][:], xrt[2][:], ALU.add))
                v.memset(dummyp[0:1, :], 0.0).then_inc(s["chunkdone"], 1)
            jlast = CH * 4 - 1
            v.wait_ge(s["tp"], jlast + 1)
            for ft in range(4 + NKT):
                ins = v.tensor_copy(qkstage[:, ft * 512 + 3 * 128:ft * 512 + 3 * 128 + 128],
                                    tp[jlast % 2][:, ft * 128:(ft + 1) * 128])
            ins.then_inc(s["tpev"], 1)

        @block.gpsimd
        def _(g):
            for cs in range(CH):
                g.wait_ge(s["tpev"], 4 * (cs + 1))
                g.wait_ge(s["chunkdone"], cs + 1)
                for ft in range(4):
                    g.dma_start(qT_o[ft * 128:(ft + 1) * 128, cs * 512:(cs + 1) * 512],
                                qkstage[:, ft * 512:(ft + 1) * 512]).then_inc(s["store"], 16)
                for kt in range(NKT):
                    g.dma_start(kT_o[kt * 128:(kt + 1) * 128, cs * 512:(cs + 1) * 512],
                                qkstage[:, (4 + kt) * 512:(5 + kt) * 512]
                                ).then_inc(s["store"], 16)
                for stx in range(4):
                    jj = 4 * cs + stx
                    if layer == 0:
                        for hh in range(4):
                            g.dma_start(v_o[hh, jj],
                                        vstage[:, stx * 512 + hh * 128:
                                               stx * 512 + (hh + 1) * 128]
                                        ).then_inc(s["store"], 16)
                    else:
                        g.dma_start(v_o[0, jj], vstage[:, stx * 128:(stx + 1) * 128]
                                    ).then_inc(s["store"], 16)
                if layer == 0:
                    for rt in range(4):
                        g.dma_start(xred_o[rt * 128:(rt + 1) * 128, cs * 512:(cs + 1) * 512],
                                    xred[:, rt * 512:(rt + 1) * 512]).then_inc(s["store"], 16)
                else:
                    g.dma_start(ffn_o[cs * 512:(cs + 1) * 512, :].rearrange(
                        "(j p) f -> p j f", p=128),
                        ffnstage[:].rearrange("p (j f) -> p j f", f=512)
                    ).then_inc(s["store"], 16)
            g.wait_ge(s["store"], n_stores * 16 * CH)
    NT = CH * 4
    tot = {s["wl"]: WL_FINAL, s["ident"]: 16, s["sel"]: SEL_FINAL, s["vch"]: VCH[0],
           s["mm"]: NT, s["qn"]: NT, s["evt"]: NT, s["tp"]: NT, s["tpev"]: NT,
           s["store"]: n_stores * 16 * CH, s["chunkdone"]: CH,
           s["xnld0"]: N_LD * 16 * ((CH + 1) // 2), s["xnld1"]: N_LD * 16 * (CH // 2)}
    if layer == 1:
        tot[s["ct0"]] = 48
        tot[s["ct1"]] = 48
        tot[s["prep"]] = 6
        tot[s["dq"]] = 6
        tot[s["tpw"]] = 192
        tot[s["tpc"]] = 192
        tot[s["mm1"]] = NT
        tot[s["t1ev"]] = NT
        tot[s["tt"]] = NT
        tot[s["t1T"]] = NT
    return tot


def emit_attn(nc, block, st0, s, layer, pfx, io):
    slots = L0_SLOTS if layer == 0 else L1_SLOTS
    nkv = 4 if layer == 0 else 1
    qT = io["qT"]
    kT_i = io["kT"]
    v_i = io["v4"]
    ident_in = io["ident"]
    qsubs_used = sorted({q for q, _ in slots})
    masks_in = {qsub: io[f"mask{qsub}"] for qsub in qsubs_used}
    if layer == 0:
        slope_in = io["slope_bc"]
        kpos_in = io["kpos"]
    else:
        mcol_in = io["mcol"]
    ctxT_o = io["ctxT"]

    sched = []
    for sl, (qsub, win) in enumerate(slots):
        if _ATTN_TRUNC is not None and sl >= _ATTN_TRUNC[0]:
            continue
        qlist = []
        for qs in range(0, S, qsub):
            ks_hi = (qs + qsub - 1) // 128
            ks_lo = 0 if win is None else max(0, (qs - win) // 128)
            kl = [(ks, (qs - ks * 128) if (qs - ks * 128) < 128 else None)
                  for ks in range(ks_lo, ks_hi + 1)]
            qlist.append((qs, kl))
        if _ATTN_TRUNC is not None:
            qlist = qlist[:_ATTN_TRUNC[1]]
        sched.append((sl, qsub, win, qlist))
    mask_off = {64: {64: 0, 0: 1}, 256: {0: 0, -128: 1},
                512: {0: 0, -128: 1, -256: 2, -384: 3}}
    exp_at_qsub_end = []
    tot = 0
    for sl, qsub, win, qlist in sched:
        for qs, kl in qlist:
            tot += len(kl)
            exp_at_qsub_end.append(tot)

    with ExitStack() as st:
        sb = lambda n, s_, d: st.enter_context(nc.sbuf_tensor(pfx + n, s_, d))
        psum = lambda n, s_, d: st.enter_context(nc.psum_tensor(pfx + n, s_, d))
        nbuf = 2 if layer == 0 else 1
        kv_k = [sb(f"kvk{i}", [128, S], BF16) for i in range(nbuf)]
        kv_v = [sb(f"kvv{i}", [128, 64 * 132], BF16) for i in range(nbuf)]
        q_sb = [sb(f"qsb{i}", [128, S], BF16) for i in range(2)]
        probs = [sb(f"pr{i}", [128, 512], BF16) for i in range(2)]
        ctxnat = sb("ctxnat", [128, 512], BF16)
        ctstage = [sb(f"cts{i}", [128, 512], BF16) for i in range(2)]
        ident = sb("ident_sb", [128, 128], BF16)
        mask_sb = {}
        for qsub in qsubs_used:
            nD = {64: 2, 256: 2, 512: 4}[qsub]
            mask_sb[qsub] = sb(f"msk{qsub}", [128, nD * qsub], F32)
        if layer == 0:
            slope_sb = sb("slope_sb", [128, 4], F32)
            kpos_sb = sb("kpos_sb", [128, 64], F32)
            btab = sb("btab", [128, 64], F32)
            ctmp = sb("ctmp", [128, 1], F32)
        else:
            mcol_sb = sb("mcol_sb", [128, 1], F32)
        rc = sb("rc", [128, 1], F32)
        dummy = sb("dummy_a", [1, 8], F32)
        scps = [psum(f"scps{i}", [128, 512], F32) for i in range(2)]
        ctxps = [psum(f"ctxps{i}", [128, 512], F32) for i in range(4)]
        ctp = psum("ctp", [128, 1024], BF16)

        lds_p = [s["lds0"], s["lds1"]]
        out_p = [s["out0"], s["out1"]]

        nslots = len(sched)
        n_const = 1 + len(qsubs_used) + (2 if layer == 0 else 1)
        ld_gate = []
        ptot = [0, 0]
        for sl in range(nslots):
            n = (66 if (layer == 0 or sl == 0) else 1)
            ptot[sl % 2] += 16 * n
            ld_gate.append((sl % 2, ptot[sl % 2]))

        @block.sync
        def _(sync):
            sync.dma_start(ident[:], ident_in).then_inc(s["const"], 16)
            for qsub in qsubs_used:
                sync.dma_start(mask_sb[qsub][:], masks_in[qsub]).then_inc(s["const"], 16)
            if layer == 0:
                sync.dma_start(slope_sb[:], slope_in).then_inc(s["const"], 16)
                sync.dma_start(kpos_sb[:], kpos_in).then_inc(s["const"], 16)
            else:
                sync.dma_start(mcol_sb[:], mcol_in).then_inc(s["const"], 16)
            for sl in range(nslots):
                if sl >= 2:
                    sync.wait_ge(s["slotdone"], sl - 1)
                par = sl % nbuf
                if layer == 0 or sl == 0:
                    sync.dma_start(kv_k[par][:], kT_i[sl * 128:(sl + 1) * 128, :]
                                   ).then_inc(lds_p[sl % 2], 16)
                    for j in range(64):
                        sync.dma_start(kv_v[par][:, j * 132:j * 132 + 128],
                                       v_i[sl if layer == 0 else 0, j]
                                       ).then_inc(lds_p[sl % 2], 16)
                sync.dma_start(q_sb[sl % 2][:], qT[sl * 128:(sl + 1) * 128, :]
                               ).then_inc(lds_p[sl % 2], 16)

        @block.tensor
        def _(t):
            ksc = 0
            nq = 0
            for sl, qsub, win, qlist in sched:
                par = sl % nbuf
                t.wait_ge(s["const"], 16 * n_const)
                t.wait_ge(lds_p[ld_gate[sl][0]], ld_gate[sl][1])
                t.wait_ge(s["ones"], min(sl + 1, nbuf) if layer == 0 else 1)
                nqt = (qsub + 127) // 128
                for qs, kl in qlist:
                    nv = len(kl)
                    for m, (ks, D) in enumerate(kl):
                        if ksc >= 2:
                            t.wait_ge(s["exp"], ksc - 1)
                        t.matmul(scps[ksc % 2][:, 0:qsub],
                                 kv_k[par][:, ks * 128:(ks + 1) * 128],
                                 q_sb[sl % 2][:, qs:qs + qsub],
                                 start=True, stop=True).then_inc(s["sc"], 1)
                        ksc += 1
                        if m >= 1:
                            t.wait_ge(s["exp"], ksc - 1)
                            pks = kl[m - 1][0]
                            for qt in range(nqt):
                                mp = min(128, qsub - qt * 128)
                                ins = t.matmul(
                                    ctxps[qt][0:mp, 0:129],
                                    probs[(ksc - 2) % 2][:, qt * 128:qt * 128 + mp],
                                    kv_v[par][:, pks * 132:pks * 132 + 129],
                                    start=(m == 1), stop=False)
                            ins.then_inc(s["pb"], 1)
                    t.wait_ge(s["exp"], ksc)
                    lks = kl[-1][0]
                    for qt in range(nqt):
                        mp = min(128, qsub - qt * 128)
                        ins = t.matmul(ctxps[qt][0:mp, 0:129],
                                       probs[(ksc - 1) % 2][:, qt * 128:qt * 128 + mp],
                                       kv_v[par][:, lks * 132:lks * 132 + 129],
                                       start=(nv == 1), stop=True)
                    ins.then_inc(s["pb"], 1)
                    nq += 1
                    t.wait_ge(s["cv"], nq)
                    if nq >= 2:
                        t.wait_ge(s["ctev"], nq - 1)
                    for qt in range(nqt):
                        ins = t.transpose(ctp[:, qt * 128:(qt + 1) * 128],
                                          ctxnat[:, qt * 128:(qt + 1) * 128], ident[:])
                    ins.then_inc(s["ctp"], 1)

        NCH_F = [0]

        @block.vector
        def _(v):
            nch = NCH_F
            def edge(ins):
                nch[0] += 1
                ins.then_inc(s["vch"], 1)
                v.wait_ge(s["vch"], nch[0])
            ksc = 0
            nq = 0
            for sl, qsub, win, qlist in sched:
                par = sl % nbuf
                if (layer == 0 and sl < nbuf) or (layer == 1 and sl == 0):
                    v.wait_ge(s["const"], 16 * n_const)
                    v.wait_ge(lds_p[ld_gate[sl][0]], ld_gate[sl][1])
                    v.memset(kv_v[par][:].rearrange("p (j d) -> p j d", d=132)
                             [:, :, 128:132], 1.0).then_inc(s["ones"], 1)
                nqt = (qsub + 127) // 128
                for qs, kl in qlist:
                    if layer == 0:
                        if nq >= 1:
                            v.wait_ge(s["exp"], exp_at_qsub_end[nq - 1])
                        qref = float(qs + qsub - 1)
                        edge(v.tensor_scalar(ctmp[:], slope_sb[:, sl:sl + 1], -qref,
                                             -M_STAB, ALU.mult, ALU.add))
                        v.tensor_scalar(btab[:], kpos_sb[:], slope_sb[:, sl:sl + 1],
                                        ctmp[:], ALU.mult, ALU.add).then_inc(s["bt"], 1)
                    for m, (ks, D) in enumerate(kl):
                        if D is not None:
                            v.wait_ge(s["sc"], ksc + 1)
                            off = mask_off[qsub][D] * qsub
                            v.tensor_tensor(scps[ksc % 2][:, 0:qsub],
                                            scps[ksc % 2][:, 0:qsub],
                                            mask_sb[qsub][:, off:off + qsub],
                                            ALU.add).then_inc(s["mk"], 1)
                        ksc += 1
                    v.wait_ge(s["pb"], exp_at_qsub_end[nq])
                    for qt in range(nqt):
                        mp = min(128, qsub - qt * 128)
                        edge(v.reciprocal(rc[0:mp, :], ctxps[qt][0:mp, 128:129]))
                        edge(v.tensor_scalar(ctxnat[0:mp, qt * 128:(qt + 1) * 128],
                                             ctxps[qt][0:mp, 0:128], rc[0:mp, :],
                                             None, ALU.mult))
                    v.memset(dummy[0:1, :], 0.0).then_inc(s["cv"], 1)
                    v.wait_ge(s["ctp"], nq + 1)
                    if nq >= 2:
                        v.wait_ge(out_p[nq % 2], 16 * (nq // 2))
                    v.tensor_copy(ctstage[nq % 2][:, 0:qsub], ctp[:, 0:qsub]
                                  ).then_inc(s["ctev"], 1)
                    nq += 1
                v.memset(dummy[0:1, :], 0.0).then_inc(s["slotdone"], 1)

        @block.scalar
        def _(sc):
            ksc = 0
            nmk = 0
            nbt = 0
            for sl, qsub, win, qlist in sched:
                for qs, kl in qlist:
                    if layer == 0:
                        nbt += 1
                        sc.wait_ge(s["bt"], nbt)
                    for m, (ks, D) in enumerate(kl):
                        if D is not None:
                            nmk += 1
                            sc.wait_ge(s["mk"], nmk)
                        else:
                            sc.wait_ge(s["sc"], ksc + 1)
                        if ksc >= 2:
                            sc.wait_ge(s["pb"], ksc - 1)
                        bias = btab[:, ks:ks + 1] if layer == 0 else mcol_sb[:]
                        sc.activation(probs[ksc % 2][:, 0:qsub], scps[ksc % 2][:, 0:qsub],
                                      AF.Exp, bias=bias, scale=SCALE
                                      ).then_inc(s["exp"], 1)
                        ksc += 1

        @block.gpsimd
        def _(g):
            nq = 0
            for sl, qsub, win, qlist in sched:
                for qs, kl in qlist:
                    g.wait_ge(s["ctev"], nq + 1)
                    g.dma_start(ctxT_o[sl * 128:(sl + 1) * 128, qs:qs + qsub],
                                ctstage[nq % 2][:, 0:qsub]).then_inc(out_p[nq % 2], 16)
                    nq += 1
            g.wait_ge(out_p[0], 16 * ((nq + 1) // 2))
            g.wait_ge(out_p[1], 16 * (nq // 2))
    n_ks = sum(len(kl) for _, _, _, ql in sched for _, kl in ql)
    n_mk = sum(1 for _, _, _, ql in sched for _, kl in ql for _, D in kl
               if D is not None)
    n_q = sum(len(ql) for _, _, _, ql in sched)
    n_vch = (2 * n_q if layer == 0 else 0) + 2 * n_q
    tot = {s["const"]: 16 * n_const, s["lds0"]: ptot[0], s["lds1"]: ptot[1],
           s["sc"]: n_ks, s["mk"]: n_mk, s["exp"]: n_ks, s["pb"]: n_ks,
           s["cv"]: n_q, s["ctp"]: n_q, s["ctev"]: n_q,
           s["out0"]: 16 * ((n_q + 1) // 2), s["out1"]: 16 * (n_q // 2),
           s["slotdone"]: len(sched),
           s["ones"]: sum(1 for sl_, _, _, _ in sched
                          if (layer == 0 and sl_ < nbuf) or (layer == 1 and sl_ == 0)),
           s["vch"]: NCH_F[0]}
    if layer == 0:
        tot[s["bt"]] = n_q
    return tot


def emit_out(nc, block, st0, s, layer, pfx, io):
    ctxT = io["ctxT"]
    x_in = io["x_cols"]
    ident_in = io["ident"]
    if layer == 0:
        w_in = io["woutp"]
        b_in = io["b_row"]
        kmat_in = io["kmat"]
        xred_in = io["xredT"]
    else:
        oc_in = io["o_codesT"]
        os_in = io["o_scalesT"]
        ffn_in = io["ffn1"]
    xout = io["xout"]
    CH = 16 if _OUT_TRUNC is None else _OUT_TRUNC

    with ExitStack() as st:
        sb = lambda n, s_, d: st.enter_context(nc.sbuf_tensor(pfx + n, s_, d))
        psum = lambda n, s_, d: st.enter_context(nc.psum_tensor(pfx + n, s_, d))
        ct_p = [s["ct0"], s["ct1"]]
        ld_p = [s["ld0"], s["ld1"]]
        store_p = [s["store0"], s["store1"]]
        W_sb = sb("W_sb", [128, 32 * 512], BF16)
        ident = sb("ident_sb", [128, 128], BF16)
        ones_sb = sb("ones_sb", [1, 128], BF16)
        dummyo = sb("dummyo", [1, 8], F32)
        psO = [psum(f"psO{i}", [128, 512], F32) for i in range(2)]
        tp = [psum(f"tp{i}", [128, 1024], BF16) for i in range(2)]

        if layer == 0:
            ksb8 = sb("ksb8", [128, 4 * 512], I8)
            ksb = sb("ksb", [128, 4 * 512], BF16)
            brow_sb = sb("brow_sb", [1, FC], BF16)

            @block.gpsimd
            def _(g):
                g.dma_start(ident[:], ident_in).then_inc(s["ident"], 16)
                g.dma_start(W_sb[:].rearrange("p (n f) -> p n f", f=512),
                            w_in.rearrange("(n p) f -> p n f", p=128)
                            ).then_inc(s["wl"], 16)
                g.dma_start(brow_sb[:], b_in).then_inc(s["wl"], 16)
                g.dma_start(ksb8[:].rearrange("p (n f) -> p n f", f=512),
                            kmat_in.rearrange("(n p) f -> p n f", p=128)
                            ).then_inc(s["wl"], 16)
            WL_FINAL = 48

            @block.scalar
            def _(sc):
                sc.wait_ge(s["wl"], WL_FINAL)
                sc.activation(ksb[:], ksb8[:], AF.Copy).then_inc(s["prep"], 1)
            PREP_FINAL = 1
            TPC_FINAL = 0
        else:
            with ExitStack() as stp:
                sbp = lambda n, s_, d: stp.enter_context(nc.sbuf_tensor(pfx + n, s_, d))
                oc_sb = [sbp(f"oc{i}", [128, H], I32) for i in range(2)]
                deq_sb = [sbp(f"dq{i}", [128, H], BF16) for i in range(2)]
                osc_sb = sbp("osc_sb", [128, 4 * 32], F32)

                @block.gpsimd
                def _(g):
                    g.dma_start(ident[:], ident_in).then_inc(s["ident"], 16)
                    g.dma_start(osc_sb[:].rearrange("p (n f) -> p n f", f=32),
                                os_in.rearrange("(n p) f -> p n f", p=128)
                                ).then_inc(s["wl"], 16)
                    for ft in range(4):
                        if ft >= 2:
                            g.wait_ge(s["prep"], ft - 1)
                        g.dma_start(oc_sb[ft % 2][:], oc_in[ft * 128:(ft + 1) * 128, :]
                                    ).then_inc(ct_p[ft % 2], 16)
                WL_FINAL = 16

                @block.scalar
                def _(sc):
                    for ft in range(4):
                        sc.wait_ge(ct_p[ft % 2], 16 * (ft // 2 + 1))
                        if ft >= 2:
                            sc.wait_ge(s["tpw"], 32 * (ft - 1))
                        sc.activation(deq_sb[ft % 2][:], oc_sb[ft % 2][:], AF.Copy,
                                      bias=-8.0).then_inc(s["prep"], 1)
                PREP_FINAL = 4

                @block.vector
                def _(v):
                    nt = 0
                    v.wait_ge(s["wl"], 16)
                    for ft in range(4):
                        v.wait_ge(s["prep"], ft + 1)
                        for ht in range(32):
                            ins = v.tensor_scalar(
                                deq_sb[ft % 2][:, ht * 128:(ht + 1) * 128],
                                deq_sb[ft % 2][:, ht * 128:(ht + 1) * 128],
                                osc_sb[:, ft * 32 + ht:ft * 32 + ht + 1], None, ALU.mult)
                        ins.then_inc(s["dq"], 1)
                        for ht in range(32):
                            v.wait_ge(s["tpw"], nt + 1)
                            v.tensor_copy(
                                W_sb[:, ht * 512 + ft * 128:ht * 512 + (ft + 1) * 128],
                                tp[nt % 2][:, 0:128]).then_inc(s["tpc"], 1)
                            nt += 1

                @block.tensor
                def _(t):
                    t.wait_ge(s["ident"], 16)
                    nt = 0
                    for ft in range(4):
                        t.wait_ge(s["dq"], ft + 1)
                        for ht in range(32):
                            if nt >= 2:
                                t.wait_ge(s["tpc"], nt - 1)
                            t.transpose(tp[nt % 2][:, 0:128],
                                        deq_sb[ft % 2][:, ht * 128:(ht + 1) * 128],
                                        ident[:]).then_inc(s["tpw"], 1)
                            nt += 1
            TPC_FINAL = 128

        ctx_sb = [sb(f"cx{i}", [128, 32 * 512], BF16) for i in range(2)]
        x_sb = [sb(f"x{i}", [128, 4 * 512], F32) for i in range(2)]
        out_sb = [sb(f"o{i}", [128, 4 * 512], F32) for i in range(2)]
        if layer == 0:
            xr_sb = [sb(f"xr{i}", [128, 4 * 512], BF16) for i in range(2)]
        else:
            ffn_sb = [sb(f"fn{i}", [128, 4 * 512], BF16) for i in range(2)]

        @block.vector
        def _(v):
            v.memset(ones_sb[:], 1.0).then_inc(s["ones"], 1)

        def gates(e):
            e.wait_ge(s["wl"], WL_FINAL)
            e.wait_ge(s["ident"], 16)
            e.wait_ge(s["prep"], PREP_FINAL)
            e.wait_ge(s["ones"], 1)
            if layer == 1:
                e.wait_ge(s["tpc"], TPC_FINAL)

        N_LD = 3

        @block.sync
        def _(sync):
            gates(sync)
            for cs in range(CH):
                if cs >= 2:
                    sync.wait_ge(s["chunkdone"], cs - 1)
                sync.dma_start(ctx_sb[cs % 2][:].rearrange("p (n f) -> p n f", f=512),
                               ctxT[:, cs * 512:(cs + 1) * 512].rearrange(
                                   "(n p) f -> p n f", p=128)).then_inc(ld_p[cs % 2], 16)
                sync.dma_start(x_sb[cs % 2][:].rearrange("p (n f) -> p n f", f=512),
                               x_in[cs * 512:(cs + 1) * 512, :].rearrange(
                                   "(j p) f -> p j f", p=128)).then_inc(ld_p[cs % 2], 16)
                if layer == 0:
                    sync.dma_start(xr_sb[cs % 2][:].rearrange("p (n f) -> p n f", f=512),
                                   xred_in[:, cs * 512:(cs + 1) * 512].rearrange(
                                       "(n p) f -> p n f", p=128)).then_inc(ld_p[cs % 2], 16)
                else:
                    sync.dma_start(ffn_sb[cs % 2][:].rearrange("p (n f) -> p n f", f=512),
                                   ffn_in[cs * 512:(cs + 1) * 512, :].rearrange(
                                       "(j p) f -> p j f", p=128)).then_inc(ld_p[cs % 2], 16)

        @block.tensor
        def _(t):
            gates(t)
            for cs in range(CH):
                t.wait_ge(ld_p[cs % 2], N_LD * 16 * (cs // 2 + 1))
                for stx in range(4):
                    i = 4 * cs + stx
                    par = i % 2
                    if i >= 2:
                        t.wait_ge(s["ev"], i - 1)
                    for ht in range(32):
                        ins = t.matmul(psO[par][:],
                                       ctx_sb[cs % 2][:, ht * 512 + stx * 128:
                                                      ht * 512 + stx * 128 + 128],
                                       W_sb[:, ht * 512:(ht + 1) * 512],
                                       start=(ht == 0), stop=(layer == 1 and ht == 31))
                    if layer == 0:
                        for rt in range(4):
                            t.matmul(psO[par][:],
                                     xr_sb[cs % 2][:, rt * 512 + stx * 128:
                                                   rt * 512 + stx * 128 + 128],
                                     ksb[:, rt * 512:(rt + 1) * 512],
                                     start=False, stop=False)
                        t.matmul(psO[par][:], ones_sb[:], brow_sb[:],
                                 start=False, stop=True).then_inc(s["mm"], 1)
                    else:
                        ins.then_inc(s["mm"], 1)

        nch_f = [0]

        @block.vector
        def _(v):
            nch = nch_f
            for cs in range(CH):
                for stx in range(4):
                    i = 4 * cs + stx
                    par = i % 2
                    v.wait_ge(s["mm"], i + 1)
                    if cs >= 2 and stx == 0:
                        v.wait_ge(store_p[cs % 2], 16 * (cs // 2))
                    ins = v.tensor_tensor(out_sb[cs % 2][:, stx * 512:(stx + 1) * 512],
                                          psO[par][:],
                                          x_sb[cs % 2][:, stx * 512:(stx + 1) * 512],
                                          ALU.add)
                    if layer == 1:
                        nch[0] += 1
                        ins.then_inc(s["vch"], 1)
                        v.wait_ge(s["vch"], nch[0])
                        v.tensor_tensor(out_sb[cs % 2][:, stx * 512:(stx + 1) * 512],
                                        out_sb[cs % 2][:, stx * 512:(stx + 1) * 512],
                                        ffn_sb[cs % 2][:, stx * 512:(stx + 1) * 512],
                                        ALU.add).then_inc(s["ev"], 1)
                    else:
                        ins.then_inc(s["ev"], 1)
                v.memset(dummyo[0:1, :], 0.0).then_inc(s["chunkdone"], 1)

        @block.gpsimd
        def _(g):
            for cs in range(CH):
                g.wait_ge(s["ev"], 4 * (cs + 1))
                g.dma_start(xout[cs * 512:(cs + 1) * 512, :].rearrange(
                    "(j p) f -> p j f", p=128),
                    out_sb[cs % 2][:].rearrange("p (j f) -> p j f", f=512)
                ).then_inc(store_p[cs % 2], 16)
            g.wait_ge(store_p[0], 16 * ((CH + 1) // 2))
            g.wait_ge(store_p[1], 16 * (CH // 2))
    tot = {s["wl"]: WL_FINAL, s["ident"]: 16, s["prep"]: PREP_FINAL, s["ones"]: 1,
           s["mm"]: CH * 4, s["ev"]: CH * 4, s["chunkdone"]: CH, s["vch"]: nch_f[0],
           s["ld0"]: 3 * 16 * ((CH + 1) // 2), s["ld1"]: 3 * 16 * (CH // 2),
           s["store0"]: 16 * ((CH + 1) // 2), s["store1"]: 16 * (CH // 2)}
    if layer == 1:
        tot[s["ct0"]] = 32
        tot[s["ct1"]] = 32
        tot[s["dq"]] = 4
        tot[s["tpw"]] = 128
        tot[s["tpc"]] = 128
    return tot




def emit_lnphase(nc, block, st0, s, pfx, io, rg):
    x_ap = io["x_cols"]
    eps_in = io["epscol"]
    ident_in = io["ident"]
    arin = io["arin"]
    arout = io["arout"]
    agin = io["agin"]
    agout = io["agout"]
    T = 64
    with ExitStack() as st:
        sb = lambda n, s_, d: st.enter_context(nc.sbuf_tensor(pfx + n, s_, d))
        psum = lambda n, s_, d: st.enter_context(nc.psum_tensor(pfx + n, s_, d))
        xb = [sb(f"xb{i}", [128, 512], F32) for i in range(2)]
        sq = [sb(f"sq{i}", [128, 512], F32) for i in range(2)]
        stat = sb("stat", [128, 128], F32)
        statg = sb("statg", [128, 128], F32)
        meanv = sb("meanv", [128, 64], F32)
        ex2 = sb("ex2", [128, 64], F32)
        negm2 = sb("negm2", [128, 64], F32)
        varv = sb("varv", [128, 64], F32)
        lnv = sb("lnv", [128, 64], F32)
        rstd = sb("rstd", [128, 64], F32)
        negmr = sb("negmr", [128, 64], F32)
        eps_sb = sb("eps", [128, 1], F32)
        ident = sb("ident", [128, 128], BF16)
        xnb = [sb(f"xnb{i}", [128, 512], BF16) for i in range(2)]
        xstage = sb("xstage", [128, 4 * 512], BF16)
        dummy = sb("dummy", [1, 8], F32)
        tp = [psum(f"tp{i}", [128, 1024], BF16) for i in range(2)]
        x_p = [s["x0"], s["x1"]]
        x2_p = [s["x20"], s["x21"]]

        @block.sync
        def _(sync):
            sync.dma_start(eps_sb[:], eps_in).then_inc(s["cst"], 16)
            sync.dma_start(ident[:], ident_in).then_inc(s["cst"], 16)
            for t in range(T):
                if t >= 2:
                    sync.wait_ge(s["sq"], t - 1)
                    sync.wait_ge(s["red"], t - 1)
                sync.dma_start(xb[t % 2][:], x_ap[t * 128:(t + 1) * 128, :]
                               ).then_inc(x_p[t % 2], 16)
            sync.wait_ge(s["stats"], 1)
            for t in range(T):
                if t >= 2:
                    sync.wait_ge(s["xn"], t - 1)
                sync.dma_start(xb[t % 2][:], x_ap[t * 128:(t + 1) * 128, :]
                               ).then_inc(x2_p[t % 2], 16)

        @block.vector
        def _(v):
            nch = [0]
            def edge(ins):
                nch[0] += 1
                ins.then_inc(s["vch"], 1)
                v.wait_ge(s["vch"], nch[0])
            for t in range(T):
                v.wait_ge(x_p[t % 2], 16 * (t // 2 + 1))
                v.reduce_sum(stat[:, t:t + 1], xb[t % 2][:], axis=AX.X
                             ).then_inc(s["red"], 1)
            v.wait_ge(s["stats"], 1)
            edge(v.tensor_copy(statg[:], stat[:]) if False else
                 v.tensor_scalar(meanv[:], statg[:, 0:64], 1.0 / H, None, ALU.mult))
            edge(v.tensor_scalar(ex2[:], statg[:, 64:128], 1.0 / H, None, ALU.mult))
            edge(v.scalar_tensor_tensor(negm2[:], meanv[:], -1.0, meanv[:],
                                        ALU.mult, ALU.mult))
            edge(v.tensor_tensor(varv[:], ex2[:], negm2[:], ALU.add))
            v.wait_ge(s["cst"], 32)
            v.tensor_scalar(varv[:], varv[:], eps_sb[:], None, ALU.add
                            ).then_inc(s["var"], 1)
            v.wait_ge(s["sqrt"], 1)
            edge(v.tensor_copy(rstd[:], lnv[:]))
            v.scalar_tensor_tensor(negmr[:], meanv[:], -1.0, rstd[:],
                                   ALU.mult, ALU.mult).then_inc(s["rn"], 1)
            v.wait_ge(s["rn"], 1)
            for t in range(T):
                v.wait_ge(x2_p[t % 2], 16 * (t // 2 + 1))
                v.tensor_scalar(xnb[t % 2][:], xb[t % 2][:], rstd[:, t:t + 1],
                                negmr[:, t:t + 1], ALU.mult, ALU.add
                                ).then_inc(s["xn"], 1)
                if t >= 1:
                    j = t - 1
                    v.wait_ge(s["tp"], j + 1)
                    if j % 4 == 0 and j >= 4:
                        v.wait_ge(s["ag"], 16 * 4 * (j // 4))
                    for ft in range(4):
                        ins = v.tensor_copy(
                            xstage[:, ft * 512 + (j % 4) * 128:
                                   ft * 512 + (j % 4) * 128 + 128],
                            tp[j % 2][:, ft * 128:(ft + 1) * 128])
                    ins.then_inc(s["tpev"], 1)
            j = T - 1
            v.wait_ge(s["tp"], T)
            for ft in range(4):
                ins = v.tensor_copy(xstage[:, ft * 512 + (j % 4) * 128:
                                           ft * 512 + (j % 4) * 128 + 128],
                                    tp[j % 2][:, ft * 128:(ft + 1) * 128])
            ins.then_inc(s["tpev"], 1)
            NCHL[0] = nch[0]

        @block.scalar
        def _(sc):
            for t in range(T):
                sc.wait_ge(x_p[t % 2], 16 * (t // 2 + 1))
                sc.activation(sq[t % 2][:], xb[t % 2][:], AF.Square,
                              accum_out=stat[:, 64 + t:64 + t + 1]).then_inc(s["sq"], 1)
            sc.wait_ge(s["var"], 1)
            sc.activation(lnv[:], varv[:], AF.Ln).then_inc(s["ln"], 1)
            sc.wait_ge(s["ln"], 1)
            sc.activation(lnv[:], lnv[:], AF.Exp, scale=-0.5).then_inc(s["sqrt"], 1)

        @block.tensor
        def _(t_):
            t_.wait_ge(s["cst"], 32)
            for t in range(T):
                t_.wait_ge(s["xn"], t + 1)
                if t >= 2:
                    t_.wait_ge(s["tpev"], t - 1)
                for ft in range(4):
                    ins = t_.transpose(tp[t % 2][:, ft * 128:(ft + 1) * 128],
                                       xnb[t % 2][:, ft * 128:(ft + 1) * 128], ident[:])
                ins.then_inc(s["tp"], 1)

        @block.gpsimd
        def _(g):
            g.wait_ge(s["red"], 64)
            g.wait_ge(s["sq"], 64)
            g.dma_start(arin, stat[:]).then_inc(s["arx"], 16)
            g.wait_ge(s["arx"], 16)
            g.collective_compute("AllReduce", ALU.add, replica_groups=rg,
                                 ins=[arin], outs=[arout]).then_inc(s["arc"], 1)
            g.wait_ge(s["arc"], 1)
            g.dma_start(statg[:], arout).then_inc(s["arx"], 16)
            g.wait_ge(s["arx"], 32)
            g.sem_inc(s["stats"], 1)
            for grp in range(16):
                g.wait_ge(s["tpev"], 4 * (grp + 1))
                for ft in range(4):
                    g.dma_start(agin[ft * 128:(ft + 1) * 128,
                                     grp * 512:(grp + 1) * 512],
                                xstage[:, ft * 512:(ft + 1) * 512]).then_inc(s["ag"], 16)
            g.wait_ge(s["ag"], 16 * 64)
            g.collective_compute("AllGather", ALU.bypass, replica_groups=rg,
                                 ins=[agin], outs=[agout]).then_inc(s["agc"], 1)
            g.wait_ge(s["agc"], 1)
    tot = {s["cst"]: 32, s["x0"]: 16 * 32, s["x1"]: 16 * 32, s["x20"]: 16 * 32,
           s["x21"]: 16 * 32, s["sq"]: 64, s["red"]: 64, s["var"]: 1, s["ln"]: 1,
           s["sqrt"]: 1, s["rn"]: 1, s["xn"]: 64, s["tp"]: 64, s["tpev"]: 64,
           s["ag"]: 16 * 64, s["agc"]: 1, s["arx"]: 32, s["arc"]: 1, s["stats"]: 1,
           s["vch"]: NCHL[0]}
    return tot


NCHL = [0]

LN_SEMS = ["cst", "x0", "x1", "x20", "x21", "sq", "red", "var", "ln", "sqrt", "rn",
           "xn", "tp", "tpev", "ag", "agc", "arx", "arc", "stats", "vch",
           "agc1", "arc1"]
PROJ_SEMS = ["wl", "ident", "ct0", "ct1", "prep", "dq", "tpw", "tpc", "xnld0",
             "xnld1", "mm", "mm1", "t1ev", "tt", "t1T", "qn", "evt", "tp", "tpev",
             "store", "chunkdone", "sel", "vch"]
ATTN_SEMS = ["const", "lds0", "lds1", "sc", "mk", "bt", "exp", "pb", "cv", "ctp",
             "ctev", "out0", "out1", "slotdone", "ones", "vch", "agc", "agc1"]
OUT_SEMS = ["wl", "ident", "ct0", "ct1", "prep", "dq", "tpw", "tpc", "ld0", "ld1",
            "mm", "ev", "store0", "store1", "chunkdone", "ones", "vch"]


def build_fused():
    nc = bass.Bass("TRN2", target_bir_lowering=False)
    rg = [list(range(NC))]
    ap = lambda *a, **k: nc.dram_tensor(*a, **k).ap()
    io = {}
    io["ident"] = ap("ident", [128, 128], BF16, kind="ExternalInput")
    io["epscol"] = ap("epscol", [128, 1], F32, kind="ExternalInput")
    io["x_cols0"] = ap("x_cols0", [S, FC], F32, kind="ExternalInput")
    io["w_all"] = ap("w_all", [H, 1536], F32, kind="ExternalInput")
    io["b_row0"] = ap("b_row0", [1, 1536], F32, kind="ExternalInput")
    for q in (64, 256, 512):
        nD = {64: 2, 256: 2, 512: 4}[q]
        io[f"mask{q}"] = ap(f"mask{q}", [128, nD * q], F32, kind="ExternalInput")
    io["slope_bc"] = ap("slope_bc", [128, 4], F32, kind="ExternalInput")
    io["kpos"] = ap("kpos", [128, 64], F32, kind="ExternalInput")
    io["mcol"] = ap("mcol", [128, 1], F32, kind="ExternalInput")
    io["woutp"] = ap("woutp", [H, FC], F32, kind="ExternalInput")
    io["bout_row"] = ap("bout_row", [1, FC], F32, kind="ExternalInput")
    io["kmat"] = ap("kmat", [512, FC], I8, kind="ExternalInput")
    io["codesT_all"] = ap("codesT_all", [768, H], I32, kind="ExternalInput")
    io["scalesT_all"] = ap("scalesT_all", [768, 32], F32, kind="ExternalInput")
    io["wffnT"] = ap("wffnT", [H, FC], F32, kind="ExternalInput")
    io["bffn_row"] = ap("bffn_row", [1, FC], F32, kind="ExternalInput")
    io["loraA_arr"] = ap("loraA_arr", [128, 32 * 48], F32, kind="ExternalInput")
    io["loraB_arr"] = ap("loraB_arr", [16, 3 * FC], F32, kind="ExternalInput")
    io["lsi"] = ap("lsi", [1, 4], I32, kind="ExternalInput")
    io["cos_arr"] = ap("cos_arr", [128, 4096], F32, kind="ExternalInput")
    io["sin_arr"] = ap("sin_arr", [128, 4096], F32, kind="ExternalInput")
    io["o_codesT"] = ap("o_codesT", [FC, H], I32, kind="ExternalInput")
    io["o_scalesT"] = ap("o_scalesT", [FC, 32], F32, kind="ExternalInput")
    io["xout"] = ap("xout", [S, FC], F32, kind="ExternalOutput")
    for l in range(2):
        io[f"arin{l}"] = ap(f"arin{l}", [128, 128], F32, kind="Internal")
        io[f"arout{l}"] = ap(f"arout{l}", [128, 128], F32, kind="Internal")
        io[f"agin{l}"] = ap(f"agin{l}", [512, S], BF16, kind="Internal")
        io[f"xnT{l}"] = ap(f"xnT{l}", [H, S], BF16, kind="Internal")
        io[f"ctgin{l}"] = ap(f"ctgin{l}", [512, S], BF16, kind="Internal")
        io[f"ctxT{l}"] = ap(f"ctxT{l}", [H, S], BF16, kind="Internal")
        io[f"qT{l}"] = ap(f"qT{l}", [512, S], BF16, kind="Internal")
        io[f"kT{l}"] = ap(f"kT{l}", [(512 if l == 0 else 128), S], BF16, kind="Internal")
        io[f"v4{l}"] = ap(f"v4{l}", [(4 if l == 0 else 1), 64, 128, 128], BF16,
                          kind="Internal")
    io["xredT"] = ap("xredT", [512, S], BF16, kind="Internal")
    io["ffn1"] = ap("ffn1", [S, FC], BF16, kind="Internal")
    io["x1"] = ap("x1", [S, FC], F32, kind="Internal")

    with ExitStack() as st0:
        sems = {}
        for kind, names in (("ln", LN_SEMS), ("proj", PROJ_SEMS),
                            ("attn", ATTN_SEMS), ("out", OUT_SEMS)):
            sems[kind] = {n: st0.enter_context(nc.semaphore(f"{kind}_{n}"))
                          for n in names}
        PH = st0.enter_context(nc.semaphore("PH"))
        ph = [0]
        block = st0.enter_context(nc.Block())
        tr = _Track()
        wblock = _BlkW(block, tr)
        exp_tot = {}

        def barrier(tot):
            for sem, val in tot.items():
                exp_tot[sem] = exp_tot.get(sem, 0) + val
            @block.vector
            def _(v):
                for sem, val in tot.items():
                    v.wait_ge(sem, val + tr.base.get(sem, 0))
                v.memset(nc.sbuf_barrier_dummy[0:1, :], 0.0).then_inc(PH, 1)
            @block.sync
            def _(e):
                e.sem_inc(PH, 1)
            @block.scalar
            def _(e):
                e.sem_inc(PH, 1)
            @block.tensor
            def _(e):
                e.sem_inc(PH, 1)
            @block.gpsimd
            def _(e):
                e.sem_inc(PH, 1)
            ph[0] += 5
            tgt = ph[0]
            @block.sync
            def _(e):
                e.wait_ge(PH, tgt)
            @block.scalar
            def _(e):
                e.wait_ge(PH, tgt)
            @block.tensor
            def _(e):
                e.wait_ge(PH, tgt)
            @block.vector
            def _(e):
                e.wait_ge(PH, tgt)
            @block.gpsimd
            def _(e):
                e.wait_ge(PH, tgt)

        nphase = [0]
        with nc.sbuf_tensor("barrier_dummy", [1, 8], F32) as bd:
            nc.sbuf_barrier_dummy = bd
            for l in range(2):
                if nphase[0] >= _FUSE_PHASES:
                    break
                sfx = str(l)
                x_ap = io["x_cols0"] if l == 0 else io["x1"]
                ln_s = dict(sems["ln"])
                if l == 1:
                    ln_s["arc"] = sems["ln"]["arc1"]
                    ln_s["agc"] = sems["ln"]["agc1"]
                tot = emit_lnphase(nc, wblock, st0, ln_s, f"L{l}ln_",
                                   {"x_cols": x_ap, "epscol": io["epscol"],
                                    "ident": io["ident"], "arin": io[f"arin{l}"],
                                    "arout": io[f"arout{l}"], "agin": io[f"agin{l}"],
                                    "agout": io[f"xnT{l}"]}, rg)
                barrier(tot)
                nphase[0] += 1
                if nphase[0] >= _FUSE_PHASES:
                    break
                pio = {"xnT": io[f"xnT{l}"], "ident": io["ident"],
                       "qT": io[f"qT{l}"], "kT": io[f"kT{l}"], "v4": io[f"v4{l}"]}
                if l == 0:
                    pio.update({"w_all": io["w_all"], "b_row": io["b_row0"],
                                "xredT": io["xredT"]})
                else:
                    pio.update({k: io[k] for k in
                                ("codesT_all", "scalesT_all", "wffnT", "bffn_row",
                                 "loraA_arr", "loraB_arr", "lsi", "cos_arr",
                                 "sin_arr", "ffn1")})
                tot = emit_proj(nc, wblock, st0, sems["proj"], l, f"L{l}pj_", pio)
                barrier(tot)
                nphase[0] += 1
                if nphase[0] >= _FUSE_PHASES:
                    break
                aio = {"qT": io[f"qT{l}"], "kT": io[f"kT{l}"], "v4": io[f"v4{l}"],
                       "ident": io["ident"], "ctxT": io[f"ctgin{l}"],
                       "mask64": io["mask64"], "mask256": io["mask256"],
                       "mask512": io["mask512"]}
                if l == 0:
                    aio.update({"slope_bc": io["slope_bc"], "kpos": io["kpos"]})
                else:
                    aio.update({"mcol": io["mcol"]})
                tot = emit_attn(nc, wblock, st0, sems["attn"], l, f"L{l}at_", aio)
                agc_s = sems["attn"]["agc" if l == 0 else "agc1"]
                @block.gpsimd
                def _(g):
                    for sem, val in tot.items():
                        g.wait_ge(sem, val + tr.base.get(sem, 0))
                    g.collective_compute("AllGather", ALU.bypass, replica_groups=rg,
                                         ins=[io[f"ctgin{l}"]],
                                         outs=[io[f"ctxT{l}"]]
                                         ).then_inc(agc_s, 1)
                    g.wait_ge(agc_s, 1)
                tr.add(agc_s, 1)
                tot = dict(tot)
                tot[agc_s] = 1
                barrier(tot)
                nphase[0] += 1
                if nphase[0] >= _FUSE_PHASES:
                    break
                oio = {"ctxT": io[f"ctxT{l}"], "x_cols": x_ap, "ident": io["ident"],
                       "xout": io["x1"] if l == 0 else io["xout"]}
                if l == 0:
                    oio.update({"woutp": io["woutp"], "b_row": io["bout_row"],
                                "kmat": io["kmat"], "xredT": io["xredT"]})
                else:
                    oio.update({"o_codesT": io["o_codesT"],
                                "o_scalesT": io["o_scalesT"], "ffn1": io["ffn1"]})
                tot = emit_out(nc, wblock, st0, sems["out"], l, f"L{l}ot_", oio)
                barrier(tot)
                nphase[0] += 1
                if l == 0:
                    mism = {k: (exp_tot.get(k, 0), tr.cnt.get(k, 0))
                            for k in set(exp_tot) | set(tr.cnt)
                            if exp_tot.get(k, 0) != tr.cnt.get(k, 0)}
                    assert not mism, f"sem total mismatch after L0: {mism}"
                    tr.base = dict(tr.cnt)
    return nc



_CACHE = {}


def _get(name, fn):
    if name not in _CACHE:
        _CACHE[name] = fn()
    return _CACHE[name]


def _run(nc, in_maps):
    return run_bass_kernel_spmd(nc, in_maps, core_ids=list(range(NC)))


def _masks_np(qsub):
    Ds = {64: [64, 0], 256: [0, -128], 512: [0, -128, -256, -384]}[qsub]
    m = np.zeros((128, len(Ds) * qsub), np.float32)
    for vi, D in enumerate(Ds):
        i = np.arange(128)[:, None]
        j = np.arange(qsub)[None, :]
        m[:, vi * qsub:(vi + 1) * qsub] = np.where(i > j + D, NEG, 0.0)
    return m


def _parity_perm():
    return np.concatenate([np.arange(0, 128, 2), np.arange(1, 128, 2)])


def kernel(**inp):
    f32 = np.float32
    x0 = np.ascontiguousarray(np.asarray(inp["L0_attn_hidden_states"],
                                         f32).reshape(S, H))
    ident = np.eye(128, dtype=BF)
    slopes = _alibi_slopes(32)
    kpos = (np.arange(128)[:, None] + 128.0 * np.arange(64)[None, :]).astype(f32)
    msk = {q: _masks_np(q) for q in (64, 256, 512)}
    epscol = np.full((128, 1), 1e-5, f32)
    mcol = np.full((128, 1), -M_STAB, f32)

    wqkv = np.asarray(inp["L0_attn_w_qkv"], f32)
    bqkv = np.asarray(inp["L0_attn_b_qkv"], f32)
    wout = np.asarray(inp["L0_attn_w_out"], f32)
    bout = np.asarray(inp["L0_attn_b_out"], f32)
    kern = np.ascontiguousarray(np.asarray(inp["L0_ffn_kernel"]).reshape(512, H))
    perm = np.zeros(32, np.int64)
    for cc in range(NC):
        for j in range(4):
            perm[4 * cc + j] = cc + 8 * j
    pperm = _parity_perm()
    qc = np.asarray(inp["L1_attn_q_codes"])
    qs_ = np.asarray(inp["L1_attn_q_scales"], f32)
    kc_ = np.asarray(inp["L1_attn_k_codes"])
    ks_ = np.asarray(inp["L1_attn_k_scales"], f32)
    vc = np.asarray(inp["L1_attn_v_codes"])
    vs_ = np.asarray(inp["L1_attn_v_scales"], f32)
    wffn = np.asarray(inp["L1_ffn_weight"], f32)
    bffn = np.asarray(inp["L1_ffn_bias"], f32)
    lA = np.asarray(inp["L1_ffn_lora_A"], f32)
    lB = np.asarray(inp["L1_ffn_lora_B"], f32)
    lsi = np.ascontiguousarray(
        np.asarray(inp["L1_ffn_lora_start_indices"]).astype(np.int32).reshape(1, 4))
    oc = np.asarray(inp["L1_attn_o_codes"])
    osc = np.asarray(inp["L1_attn_o_scales"], f32)
    pos = np.arange(S, dtype=np.float64)
    invf = 1.0 / (10000.0 ** (np.arange(0, HD, 2, dtype=np.float64) / HD))
    ang = pos[:, None] * invf[None, :]
    cos_arr = np.ascontiguousarray(np.cos(ang).astype(f32).reshape(64, 128, 64)
                                   .transpose(1, 0, 2).reshape(128, 4096))
    sin_arr = np.ascontiguousarray(np.sin(ang).astype(f32).reshape(64, 128, 64)
                                   .transpose(1, 0, 2).reshape(128, 4096))
    lAarr = np.ascontiguousarray(
        lA.transpose(2, 0, 1).reshape(32, 128, 3, 16).transpose(1, 0, 2, 3)
        .reshape(128, 32 * 48))

    in_maps = []
    for c in range(NC):
        heads = [c, c + 8, c + 16, c + 24]
        qcols = np.concatenate([np.arange(h * 128, (h + 1) * 128) for h in heads])
        w_all = np.concatenate([wqkv[:, qcols], wqkv[:, H + qcols],
                                wqkv[:, 2 * H + qcols]], axis=1)
        b_row0 = np.concatenate([bqkv[qcols], bqkv[H + qcols],
                                 bqkv[2 * H + qcols]])[None, :]
        slope_bc = np.ascontiguousarray(
            np.broadcast_to(slopes[heads][None, :], (128, 4)).astype(f32))
        cols = np.arange(c * FC, (c + 1) * FC)
        woutp = np.concatenate([wout[p * 128:(p + 1) * 128, cols] for p in perm],
                               axis=0)
        qcols1 = np.concatenate([(4 * c + hh) * 128 + pperm for hh in range(4)])
        kcols1 = c * 128 + pperm
        vcols1 = np.arange(c * 128, (c + 1) * 128)
        codesT = np.ascontiguousarray(np.concatenate(
            [qc[:, qcols1].T, kc_[:, kcols1].T, vc[:, vcols1].T],
            axis=0)).astype(np.int32)
        scalesT = np.ascontiguousarray(np.concatenate(
            [qs_[:, qcols1].T, ks_[:, kcols1].T, vs_[:, vcols1].T],
            axis=0)).astype(f32)
        wffnT = np.ascontiguousarray(wffn[cols, :].T)
        lBarr = np.ascontiguousarray(
            lB[:, cols, :].transpose(0, 2, 1).transpose(1, 0, 2).reshape(16, 3 * FC))
        in_maps.append({
            "ident": ident, "epscol": epscol,
            "x_cols0": np.ascontiguousarray(x0[:, cols]),
            "w_all": np.ascontiguousarray(w_all),
            "b_row0": np.ascontiguousarray(b_row0),
            "mask64": msk[64], "mask256": msk[256], "mask512": msk[512],
            "slope_bc": slope_bc, "kpos": kpos, "mcol": mcol,
            "woutp": np.ascontiguousarray(woutp),
            "bout_row": np.ascontiguousarray(bout[cols][None, :]),
            "kmat": np.ascontiguousarray(kern[:, cols]),
            "codesT_all": codesT, "scalesT_all": scalesT, "wffnT": wffnT,
            "bffn_row": np.ascontiguousarray(bffn[cols][None, :]),
            "loraA_arr": lAarr, "loraB_arr": lBarr, "lsi": lsi,
            "cos_arr": cos_arr, "sin_arr": sin_arr,
            "o_codesT": np.ascontiguousarray(oc[:, cols].T).astype(np.int32),
            "o_scalesT": np.ascontiguousarray(osc[:, cols].T).astype(f32),
        })
    nc = _get("fused", build_fused)
    res = _run(nc, in_maps)
    x2 = np.concatenate([res.results[c]["xout"] for c in range(NC)], axis=1)
    return np.ascontiguousarray(x2.reshape(1, S, H)).astype(np.float32)


# version 04 (final)
# speedup vs baseline: 1.1274x; 1.1274x over previous
import math
import numpy as np
import ml_dtypes

import concourse.bass as bass
import concourse.mybir as mybir
from concourse.bass_utils import run_bass_kernel_spmd
from contextlib import ExitStack

F32 = mybir.dt.float32
BF16 = mybir.dt.bfloat16
I32 = mybir.dt.int32
I8 = mybir.dt.int8
AF = mybir.ActivationFunctionType
ALU = mybir.AluOpType
AX = mybir.AxisListType

NC = 8
S = 8192
H = 4096
HD = 128
ROWS = S // NC
FC = H // NC
SCALE = HD ** -0.5
M_STAB = 12.0
NEG = -1.0e9

L0_SLOTS = [(64, 768), (256, 2560), (512, None), (512, None)]
L1_SLOTS = [(512, None)] * 4
_ATTN_TRUNC = None
_PROJ_TRUNC = None
_FUSE_PHASES = 8
_OUT_TRUNC = None
_NO_CLEAR = False

BF = ml_dtypes.bfloat16


def _alibi_slopes(n):
    cp2 = 2 ** math.floor(math.log2(n))
    base = 2 ** (-(2 ** -(math.log2(cp2) - 3)))
    slopes = np.power(base, np.arange(1, 1 + cp2, dtype=np.float64))
    if cp2 != n:
        eb = 2 ** (-(2 ** -(math.log2(2 * cp2) - 3)))
        rem = min(cp2, n - cp2)
        slopes = np.concatenate(
            [slopes, np.power(eb, np.arange(1, 1 + 2 * rem, 2, dtype=np.float64))])
    return slopes.astype(np.float32)


def _sems(nc, st, names):
    return {n: st.enter_context(nc.semaphore(n)) for n in names}


class _Track:
    def __init__(self):
        self.base = {}
        self.cnt = {}

    def add(self, sem, val):
        self.cnt[sem] = self.cnt.get(sem, 0) + val


class _InsW:
    __slots__ = ("i", "tr")

    def __init__(self, i, tr):
        self.i = i
        self.tr = tr

    def then_inc(self, sem, val):
        self.tr.add(sem, val)
        self.i.then_inc(sem, val)
        return self

    def __getattr__(self, n):
        return getattr(self.i, n)


class _EngW:
    def __init__(self, e, tr):
        self.e = e
        self.tr = tr

    def wait_ge(self, sem, val, **kw):
        return self.e.wait_ge(sem, val + self.tr.base.get(sem, 0), **kw)

    def sem_inc(self, sem, val=1):
        self.tr.add(sem, val)
        return self.e.sem_inc(sem, val)

    def __getattr__(self, n):
        a = getattr(self.e, n)
        if not callable(a):
            return a
        tr = self.tr

        def w(*args, **kw):
            r = a(*args, **kw)
            return _InsW(r, tr) if hasattr(r, "then_inc") else r
        return w


class _BlkW:
    def __init__(self, b, tr):
        self.b = b
        self.tr = tr

    def _deco(self, name):
        dec = getattr(self.b, name)
        tr = self.tr

        def deco(fn):
            return dec(lambda e: fn(_EngW(e, tr)))
        return deco

    @property
    def sync(self):
        return self._deco("sync")

    @property
    def vector(self):
        return self._deco("vector")

    @property
    def scalar(self):
        return self._deco("scalar")

    @property
    def tensor(self):
        return self._deco("tensor")

    @property
    def gpsimd(self):
        return self._deco("gpsimd")



def emit_proj(nc, block, st0, s, layer, pfx, io):
    xnT = io["xnT"]
    ident_in = io["ident"]
    if layer == 0:
        WB = 1536
        w_all = io["w_all"]
        b_row = io["b_row"]
        kT_rows, nkv = 512, 4
    else:
        WB = 1296
        codesT = io["codesT_all"]
        scalesT = io["scalesT_all"]
        wffnT = io["wffnT"]
        bffn_row = io["bffn_row"]
        loraA_arr = io["loraA_arr"]
        loraB_arr = io["loraB_arr"]
        lsi = io["lsi"]
        cos_in = io["cos_arr"]
        sin_in = io["sin_arr"]
        kT_rows, nkv = 128, 1
    qT_o = io["qT"]
    kT_o = io["kT"]
    v_o = io["v4"]
    if layer == 0:
        xred_o = io["xredT"]
    else:
        ffn_o = io["ffn1"]

    NKT = 4 if layer == 0 else 1
    if layer == 0:
        n_stores = (4 + NKT) + 16 + 4
    else:
        n_stores = (4 + NKT) + 4 + 1
    CH = 16 if _PROJ_TRUNC is None else _PROJ_TRUNC

    with ExitStack() as st:
        sb = lambda n, s_, d: st.enter_context(nc.sbuf_tensor(pfx + n, s_, d))
        psum = lambda n, s_, d: st.enter_context(nc.psum_tensor(pfx + n, s_, d))
        ct_p = [s["ct0"], s["ct1"]]
        xnld_p = [s["xnld0"], s["xnld1"]]
        VCH = [0]
        def mkedge(v):
            def edge(ins):
                VCH[0] += 1
                ins.then_inc(s["vch"], 1)
                v.wait_ge(s["vch"], VCH[0])
                return ins
            return edge
        W_sb = sb("W_sb", [128, 32 * WB], BF16)
        ident = sb("ident_sb", [128, 128], BF16)
        ones_sb = sb("ones_sb", [1, 128], BF16)
        dummyp = sb("dummyp", [1, 8], F32)
        brow_sb = sb("brow_sb", [1, WB if layer == 0 else FC], BF16)
        if layer == 1:
            sc_sb = sb("sc_sb", [128, 6 * 32], F32)
            beff_sb = sb("beff_sb", [16, FC], BF16)
            selbc = sb("selbc", [128, 3], F32)
        psA = [psum(f"psA{i}", [128, 512], F32) for i in range(2)]
        psB = [psum(f"psB{i}", [128, 512], F32) for i in range(2)]
        psC = [psum(f"psC{i}", [128, 512], F32) for i in range(2)]
        tp = [psum(f"tp{i}", [128, 1024], BF16) for i in range(2)]

        if layer == 0:
            @block.gpsimd
            def _(g):
                g.dma_start(ident[:], ident_in).then_inc(s["ident"], 16)
                g.dma_start(W_sb[:].rearrange("p (n f) -> p n f", f=WB),
                            w_all.rearrange("(n p) f -> p n f", p=128)
                            ).then_inc(s["wl"], 16)
                g.dma_start(brow_sb[:], b_row).then_inc(s["wl"], 16)
            WL_FINAL = 32

            @block.vector
            def _(v):
                v.memset(ones_sb[:], 1.0).then_inc(s["sel"], 1)
            SEL_FINAL = 1
            TPC_FINAL = 0
        else:
            with ExitStack() as stp:
                sbp = lambda n, s_, d: stp.enter_context(nc.sbuf_tensor(pfx + n, s_, d))
                ct_sb = [sbp(f"ct{i}", [128, H], I32) for i in range(2)]
                deq_sb = [sbp(f"dq{i}", [128, H], BF16) for i in range(2)]
                a_sb = sbp("a_sb", [128, 32 * 48], F32)
                b3_sb = sbp("b3_sb", [16, 3 * FC], F32)
                lsi_sb = sbp("lsi_sb", [1, 4], I32)
                lsif = sbp("lsif", [1, 4], F32)
                ge = sbp("ge", [1, 4], F32)
                tsc = sbp("tsc", [1, 8], F32)
                sel_sb = sbp("sel_sb", [1, 3], BF16)
                atmp = sbp("atmp", [128, 16], F32)
                atmp2 = sbp("atmp2", [128, 16], F32)

                @block.gpsimd
                def _(g):
                    g.dma_start(ident[:], ident_in).then_inc(s["ident"], 16)
                    g.dma_start(W_sb[:].rearrange("p (n f) -> p n f", f=WB)
                                [:, :, 784:1296],
                                wffnT.rearrange("(n p) f -> p n f", p=128)
                                ).then_inc(s["wl"], 16)
                    g.dma_start(brow_sb[:], bffn_row).then_inc(s["wl"], 16)
                    g.dma_start(sc_sb[:].rearrange("p (n f) -> p n f", f=32),
                                scalesT.rearrange("(n p) f -> p n f", p=128)
                                ).then_inc(s["wl"], 16)
                    g.dma_start(a_sb[:], loraA_arr).then_inc(s["wl"], 16)
                    g.dma_start(b3_sb[:], loraB_arr).then_inc(s["wl"], 16)
                    g.dma_start(lsi_sb[:], lsi).then_inc(s["wl"], 16)
                    for ft in range(6):
                        if ft >= 2:
                            g.wait_ge(s["prep"], ft - 1)
                        g.dma_start(ct_sb[ft % 2][:], codesT[ft * 128:(ft + 1) * 128, :]
                                    ).then_inc(ct_p[ft % 2], 16)
                WL_FINAL = 16 * 6

                @block.scalar
                def _(sc):
                    for ft in range(6):
                        sc.wait_ge(ct_p[ft % 2], 16 * (ft // 2 + 1))
                        if ft >= 2:
                            sc.wait_ge(s["tpw"], 32 * (ft - 1))
                        sc.activation(deq_sb[ft % 2][:], ct_sb[ft % 2][:], AF.Copy,
                                      bias=-8.0).then_inc(s["prep"], 1)

                @block.vector
                def _(v):
                    edge = mkedge(v)
                    v.wait_ge(s["wl"], WL_FINAL)
                    edge(v.tensor_copy(lsif[:], lsi_sb[:]))
                    edge(v.tensor_single_scalar(ge[:], lsif[:], 0.0, ALU.is_le))
                    edge(v.tensor_scalar(tsc[:, 0:1], ge[:, 3:4], 3.0, None, ALU.add))
                    edge(v.tensor_tensor(tsc[:, 1:2], ge[:, 2:3], tsc[:, 0:1], ALU.mult))
                    edge(v.tensor_scalar(tsc[:, 2:3], ge[:, 1:2], 2.0, None, ALU.mult))
                    edge(v.tensor_scalar(tsc[:, 3:4], ge[:, 1:2], -1.0, 1.0, ALU.mult,
                                         ALU.add))
                    edge(v.tensor_tensor(tsc[:, 4:5], tsc[:, 3:4], ge[:, 0:1], ALU.mult))
                    edge(v.tensor_tensor(tsc[:, 5:6], tsc[:, 2:3], tsc[:, 4:5], ALU.add))
                    edge(v.tensor_scalar(tsc[:, 6:7], ge[:, 2:3], -1.0, 1.0, ALU.mult,
                                         ALU.add))
                    edge(v.tensor_tensor(tsc[:, 7:8], tsc[:, 6:7], tsc[:, 5:6], ALU.mult))
                    edge(v.tensor_tensor(tsc[:, 0:1], tsc[:, 1:2], tsc[:, 7:8], ALU.add))
                    for j in range(3):
                        edge(v.tensor_single_scalar(sel_sb[:, j:j + 1], tsc[:, 0:1],
                                                    float(j + 2), ALU.is_equal))
                    v.memset(ones_sb[:], 1.0).then_inc(s["sel"], 1)

                @block.tensor
                def _(t):
                    t.wait_ge(s["sel"], 1)
                    t.matmul(psA[0][:, 0:3], ones_sb[:], sel_sb[:],
                             start=True, stop=True).then_inc(s["sel"], 1)

                @block.vector
                def _(v):
                    edge = mkedge(v)
                    v.wait_ge(s["sel"], 2)
                    edge(v.tensor_copy(selbc[:], psA[0][:, 0:3]))
                    v.memset(dummyp[0:1, :], 0.0).then_inc(s["sel"], 1)
                    edge(v.tensor_scalar(beff_sb[0:16, :], b3_sb[0:16, 0:FC],
                                         selbc[0:16, 0:1], None, ALU.mult))
                    edge(v.scalar_tensor_tensor(beff_sb[0:16, :], b3_sb[0:16, FC:2 * FC],
                                                selbc[0:16, 1:2], beff_sb[0:16, :],
                                                ALU.mult, ALU.add))
                    v.scalar_tensor_tensor(beff_sb[0:16, :], b3_sb[0:16, 2 * FC:3 * FC],
                                           selbc[0:16, 2:3], beff_sb[0:16, :],
                                           ALU.mult, ALU.add).then_inc(s["sel"], 1)
                    for ht in range(32):
                        edge(v.tensor_scalar(atmp[:], a_sb[:, ht * 48:ht * 48 + 16],
                                             selbc[:, 0:1], None, ALU.mult))
                        edge(v.scalar_tensor_tensor(atmp2[:],
                                                    a_sb[:, ht * 48 + 16:ht * 48 + 32],
                                                    selbc[:, 1:2], atmp[:],
                                                    ALU.mult, ALU.add))
                        edge(v.scalar_tensor_tensor(
                            W_sb[:, ht * WB + 768:ht * WB + 784],
                            a_sb[:, ht * 48 + 32:ht * 48 + 48], selbc[:, 2:3], atmp2[:],
                            ALU.mult, ALU.add))
                    v.memset(dummyp[0:1, :], 0.0).then_inc(s["sel"], 1)
                    nt = 0
                    for ft in range(6):
                        v.wait_ge(s["prep"], ft + 1)
                        for ht in range(32):
                            ins = v.tensor_scalar(
                                deq_sb[ft % 2][:, ht * 128:(ht + 1) * 128],
                                deq_sb[ft % 2][:, ht * 128:(ht + 1) * 128],
                                sc_sb[:, ft * 32 + ht:ft * 32 + ht + 1], None, ALU.mult)
                        ins.then_inc(s["dq"], 1)
                        dst0 = [0, 128, 256, 384, 512, 640][ft]
                        for ht in range(32):
                            v.wait_ge(s["tpw"], nt + 1)
                            v.tensor_copy(W_sb[:, ht * WB + dst0:ht * WB + dst0 + 128],
                                          tp[nt % 2][:, 0:128]).then_inc(s["tpc"], 1)
                            nt += 1

                @block.tensor
                def _(t):
                    t.wait_ge(s["ident"], 16)
                    nt = 0
                    for ft in range(6):
                        t.wait_ge(s["dq"], ft + 1)
                        for ht in range(32):
                            if nt >= 2:
                                t.wait_ge(s["tpc"], nt - 1)
                            t.transpose(tp[nt % 2][:, 0:128],
                                        deq_sb[ft % 2][:, ht * 128:(ht + 1) * 128],
                                        ident[:]).then_inc(s["tpw"], 1)
                            nt += 1
            SEL_FINAL = 5
            TPC_FINAL = 192

        xn_sb = [sb(f"xn{i}", [128, 32 * 512], BF16) for i in range(2)]
        qnat = [sb(f"qnat{i}", [128, 512], BF16) for i in range(2)]
        knat = [sb(f"knat{i}", [128, 128 * NKT], BF16) for i in range(2)]
        qkstage = sb("qks", [128, (4 + NKT) * 512], BF16)
        vstage = sb("vs", [128, 4 * (512 if layer == 0 else 128)], BF16)
        if layer == 0:
            xred = sb("xr", [128, 4 * 512], BF16)
            xrt = [sb(f"xrt{i}", [128, 512], BF16) for i in range(4)]
        else:
            cosc = [sb(f"cosc{i}", [128, 256], F32) for i in range(2)]
            sinc = [sb(f"sinc{i}", [128, 256], F32) for i in range(2)]
            qf32 = [sb(f"qf{i}", [128, 512], F32) for i in range(2)]
            kf32 = [sb(f"kf{i}", [128, 128], F32) for i in range(2)]
            rtmp1 = sb("rtmp1", [128, 64], F32)
            rtmp2 = sb("rtmp2", [128, 64], F32)
            ffnstage = sb("fs", [128, 4 * 512], BF16)
            t1_sb = [sb(f"t1_{i}", [128, 16], BF16) for i in range(2)]
            t1T_sb = [sb(f"t1T_{i}", [16, 128], BF16) for i in range(2)]

        def prep_gates(eng):
            eng.wait_ge(s["wl"], WL_FINAL)
            eng.wait_ge(s["ident"], 16)
            eng.wait_ge(s["sel"], SEL_FINAL)
            if layer == 1:
                eng.wait_ge(s["tpc"], TPC_FINAL)

        N_LD = 1 if layer == 0 else 3

        @block.sync
        def _(sync):
            prep_gates(sync)
            for cs in range(CH):
                if cs >= 2:
                    sync.wait_ge(s["chunkdone"], cs - 1)
                if "xn_gate" in io:
                    sync.wait_ge(io["xn_gate"], cs + 1)
                sync.dma_start(
                    xn_sb[cs % 2][:].rearrange("p (n f) -> p n f", f=512),
                    xnT[cs * 4096:(cs + 1) * 4096, :].rearrange("(n p) f -> p n f",
                                                                p=128),
                ).then_inc(xnld_p[cs % 2], 16)
                if layer == 1:
                    sync.dma_start(cosc[cs % 2][:], cos_in[:, cs * 256:(cs + 1) * 256]
                                   ).then_inc(xnld_p[cs % 2], 16)
                    sync.dma_start(sinc[cs % 2][:], sin_in[:, cs * 256:(cs + 1) * 256]
                                   ).then_inc(xnld_p[cs % 2], 16)

        def emit_transposes(t, j):
            t.wait_ge(s["qn"], j + 1)
            if j >= 2:
                t.wait_ge(s["tpev"], j - 1)
            par = j % 2
            for ft in range(4):
                ins = t.transpose(tp[par][:, ft * 128:(ft + 1) * 128],
                                  qnat[par][:, ft * 128:(ft + 1) * 128], ident[:])
            for kt in range(NKT):
                ins = t.transpose(tp[par][:, (4 + kt) * 128:(5 + kt) * 128],
                                  knat[par][:, kt * 128:(kt + 1) * 128], ident[:])
            ins.then_inc(s["tp"], 1)

        @block.tensor
        def _(t):
            prep_gates(t)
            for cs in range(CH):
                for stx in range(4):
                    i = 4 * cs + stx
                    par = i % 2
                    if stx == 0:
                        t.wait_ge(xnld_p[cs % 2], N_LD * 16 * (cs // 2 + 1))
                    if i >= 2:
                        t.wait_ge(s["evt"], i - 1)
                    lhs = lambda ht: xn_sb[cs % 2][:, ht * 512 + stx * 128:
                                                   ht * 512 + stx * 128 + 128]
                    if layer == 0:
                        for ht in range(32):
                            t.matmul(psA[par][:], lhs(ht), W_sb[:, ht * WB:ht * WB + 512],
                                     start=(ht == 0), stop=False)
                            t.matmul(psB[par][:], lhs(ht),
                                     W_sb[:, ht * WB + 512:ht * WB + 1024],
                                     start=(ht == 0), stop=False)
                            t.matmul(psC[par][:], lhs(ht),
                                     W_sb[:, ht * WB + 1024:ht * WB + 1536],
                                     start=(ht == 0), stop=False)
                        t.matmul(psA[par][:], ones_sb[:], brow_sb[0:1, 0:512],
                                 start=False, stop=True)
                        t.matmul(psB[par][:], ones_sb[:], brow_sb[0:1, 512:1024],
                                 start=False, stop=True)
                        t.matmul(psC[par][:], ones_sb[:], brow_sb[0:1, 1024:1536],
                                 start=False, stop=True).then_inc(s["mm"], 1)
                    else:
                        for ht in range(32):
                            t.matmul(psA[par][:], lhs(ht), W_sb[:, ht * WB:ht * WB + 512],
                                     start=(ht == 0), stop=(ht == 31))
                            t.matmul(psB[par][:, 0:272], lhs(ht),
                                     W_sb[:, ht * WB + 512:ht * WB + 784],
                                     start=(ht == 0), stop=(ht == 31))
                            t.matmul(psC[par][:], lhs(ht),
                                     W_sb[:, ht * WB + 784:ht * WB + 1296],
                                     start=(ht == 0), stop=False)
                        t.matmul(psC[par][:], ones_sb[:], brow_sb[0:1, :],
                                 start=False, stop=False).then_inc(s["mm1"], 1)
                        t.wait_ge(s["t1ev"], i + 1)
                        if i >= 2:
                            t.wait_ge(s["tpev"], i - 1)
                        t.transpose(tp[par][0:16, 0:128], t1_sb[par][:],
                                    ident[:]).then_inc(s["tt"], 1)
                        t.wait_ge(s["t1T"], i + 1)
                        t.matmul(psC[par][:], t1T_sb[par][0:16, :], beff_sb[0:16, :],
                                 start=False, stop=True).then_inc(s["mm"], 1)
                    if i >= 1:
                        emit_transposes(t, i - 1)
            emit_transposes(t, CH * 4 - 1)

        @block.vector
        def _(v):
            edge = mkedge(v)
            for cs in range(CH):
                for stx in range(4):
                    i = 4 * cs + stx
                    par = i % 2
                    if layer == 1:
                        v.wait_ge(s["mm1"], i + 1)
                        v.tensor_copy(t1_sb[par][:], psB[par][:, 256:272]
                                      ).then_inc(s["t1ev"], 1)
                        v.wait_ge(s["tt"], i + 1)
                        v.tensor_copy(t1T_sb[par][0:16, :], tp[par][0:16, 0:128]
                                      ).then_inc(s["t1T"], 1)
                    if i >= 1:
                        j = i - 1
                        v.wait_ge(s["tp"], j + 1)
                        for ft in range(4 + NKT):
                            ins = v.tensor_copy(
                                qkstage[:, ft * 512 + (j % 4) * 128:
                                        ft * 512 + (j % 4) * 128 + 128],
                                tp[j % 2][:, ft * 128:(ft + 1) * 128])
                        ins.then_inc(s["tpev"], 1)
                    v.wait_ge(s["mm"], i + 1)
                    if stx == 0 and cs >= 1:
                        v.wait_ge(s["store"], n_stores * 16 * cs)
                    if layer == 0:
                        v.tensor_copy(qnat[par][:], psA[par][:])
                        v.tensor_copy(knat[par][:], psB[par][:]).then_inc(s["qn"], 1)
                        v.tensor_copy(vstage[:, stx * 512:(stx + 1) * 512], psC[par][:]
                                      ).then_inc(s["evt"], 1)
                    else:
                        if stx == 0:
                            v.wait_ge(xnld_p[cs % 2], N_LD * 16 * (cs // 2 + 1))
                        edge(v.tensor_copy(qf32[par][:], psA[par][:]))
                        edge(v.tensor_copy(kf32[par][:], psB[par][:, 0:128]))
                        v.tensor_copy(vstage[:, stx * 128:(stx + 1) * 128],
                                      psB[par][:, 128:256])
                        v.tensor_copy(ffnstage[:, stx * 512:(stx + 1) * 512], psC[par][:]
                                      ).then_inc(s["evt"], 1)
                        cosA = cosc[cs % 2][:, stx * 64:(stx + 1) * 64]
                        sinA = sinc[cs % 2][:, stx * 64:(stx + 1) * 64]
                        for hh in range(4):
                            qe = qf32[par][:, hh * 128:hh * 128 + 64]
                            qo = qf32[par][:, hh * 128 + 64:hh * 128 + 128]
                            v.tensor_tensor(rtmp1[:], qe, cosA, ALU.mult)
                            edge(v.tensor_tensor(rtmp2[:], qo, sinA, ALU.mult))
                            edge(v.tensor_tensor(qnat[par][:, hh * 128:hh * 128 + 64],
                                                 rtmp1[:], rtmp2[:], ALU.subtract))
                            v.tensor_tensor(rtmp1[:], qo, cosA, ALU.mult)
                            edge(v.tensor_tensor(rtmp2[:], qe, sinA, ALU.mult))
                            edge(v.tensor_tensor(qnat[par][:, hh * 128 + 64:
                                                           hh * 128 + 128],
                                                 rtmp1[:], rtmp2[:], ALU.add))
                        ke = kf32[par][:, 0:64]
                        ko = kf32[par][:, 64:128]
                        v.tensor_tensor(rtmp1[:], ke, cosA, ALU.mult)
                        edge(v.tensor_tensor(rtmp2[:], ko, sinA, ALU.mult))
                        edge(v.tensor_tensor(knat[par][:, 0:64], rtmp1[:], rtmp2[:],
                                             ALU.subtract))
                        v.tensor_tensor(rtmp1[:], ko, cosA, ALU.mult)
                        edge(v.tensor_tensor(rtmp2[:], ke, sinA, ALU.mult))
                        v.tensor_tensor(knat[par][:, 64:128], rtmp1[:], rtmp2[:],
                                        ALU.add).then_inc(s["qn"], 1)
                if layer == 0:
                    for rt in range(4):
                        blocks = [xn_sb[cs % 2][:, (4 * g + rt) * 512:
                                                (4 * g + rt + 1) * 512] for g in range(8)]
                        v.tensor_tensor(xrt[0][:], blocks[0], blocks[1], ALU.add)
                        v.tensor_tensor(xrt[1][:], blocks[2], blocks[3], ALU.add)
                        v.tensor_tensor(xrt[2][:], blocks[4], blocks[5], ALU.add)
                        edge(v.tensor_tensor(xrt[3][:], blocks[6], blocks[7], ALU.add))
                        v.tensor_tensor(xrt[0][:], xrt[0][:], xrt[1][:], ALU.add)
                        edge(v.tensor_tensor(xrt[2][:], xrt[2][:], xrt[3][:], ALU.add))
                        edge(v.tensor_tensor(xred[:, rt * 512:(rt + 1) * 512],
                                             xrt[0][:], xrt[2][:], ALU.add))
                v.memset(dummyp[0:1, :], 0.0).then_inc(s["chunkdone"], 1)
            jlast = CH * 4 - 1
            v.wait_ge(s["tp"], jlast + 1)
            for ft in range(4 + NKT):
                ins = v.tensor_copy(qkstage[:, ft * 512 + 3 * 128:ft * 512 + 3 * 128 + 128],
                                    tp[jlast % 2][:, ft * 128:(ft + 1) * 128])
            ins.then_inc(s["tpev"], 1)

        @block.gpsimd
        def _(g):
            for cs in range(CH):
                g.wait_ge(s["tpev"], 4 * (cs + 1))
                g.wait_ge(s["chunkdone"], cs + 1)
                for ft in range(4):
                    g.dma_start(qT_o[ft * 128:(ft + 1) * 128, cs * 512:(cs + 1) * 512],
                                qkstage[:, ft * 512:(ft + 1) * 512]).then_inc(s["store"], 16)
                for kt in range(NKT):
                    g.dma_start(kT_o[kt * 128:(kt + 1) * 128, cs * 512:(cs + 1) * 512],
                                qkstage[:, (4 + kt) * 512:(5 + kt) * 512]
                                ).then_inc(s["store"], 16)
                for stx in range(4):
                    jj = 4 * cs + stx
                    if layer == 0:
                        for hh in range(4):
                            g.dma_start(v_o[hh, jj],
                                        vstage[:, stx * 512 + hh * 128:
                                               stx * 512 + (hh + 1) * 128]
                                        ).then_inc(s["store"], 16)
                    else:
                        g.dma_start(v_o[0, jj], vstage[:, stx * 128:(stx + 1) * 128]
                                    ).then_inc(s["store"], 16)
                if layer == 0:
                    for rt in range(4):
                        g.dma_start(xred_o[rt * 128:(rt + 1) * 128, cs * 512:(cs + 1) * 512],
                                    xred[:, rt * 512:(rt + 1) * 512]).then_inc(s["store"], 16)
                else:
                    g.dma_start(ffn_o[cs * 512:(cs + 1) * 512, :].rearrange(
                        "(j p) f -> p j f", p=128),
                        ffnstage[:].rearrange("p (j f) -> p j f", f=512)
                    ).then_inc(s["store"], 16)
            g.wait_ge(s["store"], n_stores * 16 * CH)
    NT = CH * 4
    tot = {s["wl"]: WL_FINAL, s["ident"]: 16, s["sel"]: SEL_FINAL, s["vch"]: VCH[0],
           s["mm"]: NT, s["qn"]: NT, s["evt"]: NT, s["tp"]: NT, s["tpev"]: NT,
           s["store"]: n_stores * 16 * CH, s["chunkdone"]: CH,
           s["xnld0"]: N_LD * 16 * ((CH + 1) // 2), s["xnld1"]: N_LD * 16 * (CH // 2)}
    if layer == 1:
        tot[s["ct0"]] = 48
        tot[s["ct1"]] = 48
        tot[s["prep"]] = 6
        tot[s["dq"]] = 6
        tot[s["tpw"]] = 192
        tot[s["tpc"]] = 192
        tot[s["mm1"]] = NT
        tot[s["t1ev"]] = NT
        tot[s["tt"]] = NT
        tot[s["t1T"]] = NT
    return tot


def emit_attn(nc, block, st0, s, layer, pfx, io):
    slots = L0_SLOTS if layer == 0 else L1_SLOTS
    nkv = 4 if layer == 0 else 1
    qT = io["qT"]
    kT_i = io["kT"]
    v_i = io["v4"]
    ident_in = io["ident"]
    qsubs_used = sorted({q for q, _ in slots})
    masks_in = {qsub: io[f"mask{qsub}"] for qsub in qsubs_used}
    if layer == 0:
        slope_in = io["slope_bc"]
        kpos_in = io["kpos"]
    else:
        mcol_in = io["mcol"]
    ctxT_o = io["ctxT"]

    sched = []
    for sl, (qsub, win) in enumerate(slots):
        if _ATTN_TRUNC is not None and sl >= _ATTN_TRUNC[0]:
            continue
        qlist = []
        for qs in range(0, S, qsub):
            ks_hi = (qs + qsub - 1) // 128
            ks_lo = 0 if win is None else max(0, (qs - win) // 128)
            kl = [(ks, (qs - ks * 128) if (qs - ks * 128) < 128 else None)
                  for ks in range(ks_lo, ks_hi + 1)]
            qlist.append((qs, kl))
        if _ATTN_TRUNC is not None:
            qlist = qlist[:_ATTN_TRUNC[1]]
        sched.append((sl, qsub, win, qlist))
    mask_off = {64: {64: 0, 0: 1}, 256: {0: 0, -128: 1},
                512: {0: 0, -128: 1, -256: 2, -384: 3}}
    exp_at_qsub_end = []
    tot = 0
    for sl, qsub, win, qlist in sched:
        for qs, kl in qlist:
            tot += len(kl)
            exp_at_qsub_end.append(tot)

    with ExitStack() as st:
        sb = lambda n, s_, d: st.enter_context(nc.sbuf_tensor(pfx + n, s_, d))
        psum = lambda n, s_, d: st.enter_context(nc.psum_tensor(pfx + n, s_, d))
        nbuf = 2 if layer == 0 else 1
        kv_k = [sb(f"kvk{i}", [128, S], BF16) for i in range(nbuf)]
        kv_v = [sb(f"kvv{i}", [128, 64 * 132], BF16) for i in range(nbuf)]
        q_sb = [sb(f"qsb{i}", [128, S], BF16) for i in range(2)]
        probs = [sb(f"pr{i}", [128, 512], BF16) for i in range(2)]
        ctxnat = sb("ctxnat", [128, 512], BF16)
        ctstage = [sb(f"cts{i}", [128, 512], BF16) for i in range(2)]
        ident = sb("ident_sb", [128, 128], BF16)
        mask_sb = {}
        for qsub in qsubs_used:
            nD = {64: 2, 256: 2, 512: 4}[qsub]
            mask_sb[qsub] = sb(f"msk{qsub}", [128, nD * qsub], F32)
        if layer == 0:
            slope_sb = sb("slope_sb", [128, 4], F32)
            kpos_sb = sb("kpos_sb", [128, 64], F32)
            btab = sb("btab", [128, 64], F32)
            ctmp = sb("ctmp", [128, 1], F32)
        else:
            mcol_sb = sb("mcol_sb", [128, 1], F32)
        rc = sb("rc", [128, 1], F32)
        dummy = sb("dummy_a", [1, 8], F32)
        scps = [psum(f"scps{i}", [128, 512], F32) for i in range(2)]
        ctxps = [psum(f"ctxps{i}", [128, 512], F32) for i in range(4)]
        ctp = psum("ctp", [128, 1024], BF16)

        lds_p = [s["lds0"], s["lds1"]]
        out_p = [s["out0"], s["out1"]]

        nslots = len(sched)
        n_const = 1 + len(qsubs_used) + (2 if layer == 0 else 1)
        ld_gate = []
        ptot = [0, 0]
        for sl in range(nslots):
            n = (66 if (layer == 0 or sl == 0) else 1)
            ptot[sl % 2] += 16 * n
            ld_gate.append((sl % 2, ptot[sl % 2]))

        @block.sync
        def _(sync):
            sync.dma_start(ident[:], ident_in).then_inc(s["const"], 16)
            for qsub in qsubs_used:
                sync.dma_start(mask_sb[qsub][:], masks_in[qsub]).then_inc(s["const"], 16)
            if layer == 0:
                sync.dma_start(slope_sb[:], slope_in).then_inc(s["const"], 16)
                sync.dma_start(kpos_sb[:], kpos_in).then_inc(s["const"], 16)
            else:
                sync.dma_start(mcol_sb[:], mcol_in).then_inc(s["const"], 16)
            for sl in range(nslots):
                if sl >= 2:
                    sync.wait_ge(s["slotdone"], sl - 1)
                par = sl % nbuf
                if layer == 0 or sl == 0:
                    sync.dma_start(kv_k[par][:], kT_i[sl * 128:(sl + 1) * 128, :]
                                   ).then_inc(lds_p[sl % 2], 16)
                    for j in range(64):
                        sync.dma_start(kv_v[par][:, j * 132:j * 132 + 128],
                                       v_i[sl if layer == 0 else 0, j]
                                       ).then_inc(lds_p[sl % 2], 16)
                sync.dma_start(q_sb[sl % 2][:], qT[sl * 128:(sl + 1) * 128, :]
                               ).then_inc(lds_p[sl % 2], 16)

        @block.tensor
        def _(t):
            ksc = 0
            nq = 0
            for sl, qsub, win, qlist in sched:
                par = sl % nbuf
                t.wait_ge(s["const"], 16 * n_const)
                t.wait_ge(lds_p[ld_gate[sl][0]], ld_gate[sl][1])
                t.wait_ge(s["ones"], min(sl + 1, nbuf) if layer == 0 else 1)
                nqt = (qsub + 127) // 128
                for qs, kl in qlist:
                    nv = len(kl)
                    for m, (ks, D) in enumerate(kl):
                        if ksc >= 2:
                            t.wait_ge(s["exp"], ksc - 1)
                        t.matmul(scps[ksc % 2][:, 0:qsub],
                                 kv_k[par][:, ks * 128:(ks + 1) * 128],
                                 q_sb[sl % 2][:, qs:qs + qsub],
                                 start=True, stop=True).then_inc(s["sc"], 1)
                        ksc += 1
                        if m >= 1:
                            t.wait_ge(s["exp"], ksc - 1)
                            pks = kl[m - 1][0]
                            for qt in range(nqt):
                                mp = min(128, qsub - qt * 128)
                                ins = t.matmul(
                                    ctxps[qt][0:mp, 0:129],
                                    probs[(ksc - 2) % 2][:, qt * 128:qt * 128 + mp],
                                    kv_v[par][:, pks * 132:pks * 132 + 129],
                                    start=(m == 1), stop=False)
                            ins.then_inc(s["pb"], 1)
                    t.wait_ge(s["exp"], ksc)
                    lks = kl[-1][0]
                    for qt in range(nqt):
                        mp = min(128, qsub - qt * 128)
                        ins = t.matmul(ctxps[qt][0:mp, 0:129],
                                       probs[(ksc - 1) % 2][:, qt * 128:qt * 128 + mp],
                                       kv_v[par][:, lks * 132:lks * 132 + 129],
                                       start=(nv == 1), stop=True)
                    ins.then_inc(s["pb"], 1)
                    nq += 1
                    t.wait_ge(s["cv"], nq)
                    if nq >= 2:
                        t.wait_ge(s["ctev"], nq - 1)
                    for qt in range(nqt):
                        ins = t.transpose(ctp[:, qt * 128:(qt + 1) * 128],
                                          ctxnat[:, qt * 128:(qt + 1) * 128], ident[:])
                    ins.then_inc(s["ctp"], 1)

        NCH_F = [0]

        @block.vector
        def _(v):
            nch = NCH_F
            def edge(ins):
                nch[0] += 1
                ins.then_inc(s["vch"], 1)
                v.wait_ge(s["vch"], nch[0])
            ksc = 0
            nq = 0
            for sl, qsub, win, qlist in sched:
                par = sl % nbuf
                if (layer == 0 and sl < nbuf) or (layer == 1 and sl == 0):
                    v.wait_ge(s["const"], 16 * n_const)
                    v.wait_ge(lds_p[ld_gate[sl][0]], ld_gate[sl][1])
                    v.memset(kv_v[par][:].rearrange("p (j d) -> p j d", d=132)
                             [:, :, 128:132], 1.0).then_inc(s["ones"], 1)
                nqt = (qsub + 127) // 128
                for qs, kl in qlist:
                    if layer == 0:
                        if nq >= 1:
                            v.wait_ge(s["exp"], exp_at_qsub_end[nq - 1])
                        qref = float(qs + qsub - 1)
                        edge(v.tensor_scalar(ctmp[:], slope_sb[:, sl:sl + 1], -qref,
                                             -M_STAB, ALU.mult, ALU.add))
                        v.tensor_scalar(btab[:], kpos_sb[:], slope_sb[:, sl:sl + 1],
                                        ctmp[:], ALU.mult, ALU.add).then_inc(s["bt"], 1)
                    for m, (ks, D) in enumerate(kl):
                        if D is not None:
                            v.wait_ge(s["sc"], ksc + 1)
                            off = mask_off[qsub][D] * qsub
                            v.tensor_tensor(scps[ksc % 2][:, 0:qsub],
                                            scps[ksc % 2][:, 0:qsub],
                                            mask_sb[qsub][:, off:off + qsub],
                                            ALU.add).then_inc(s["mk"], 1)
                        ksc += 1
                    v.wait_ge(s["pb"], exp_at_qsub_end[nq])
                    for qt in range(nqt):
                        mp = min(128, qsub - qt * 128)
                        edge(v.reciprocal(rc[0:mp, :], ctxps[qt][0:mp, 128:129]))
                        edge(v.tensor_scalar(ctxnat[0:mp, qt * 128:(qt + 1) * 128],
                                             ctxps[qt][0:mp, 0:128], rc[0:mp, :],
                                             None, ALU.mult))
                    v.memset(dummy[0:1, :], 0.0).then_inc(s["cv"], 1)
                    v.wait_ge(s["ctp"], nq + 1)
                    if nq >= 2:
                        v.wait_ge(out_p[nq % 2], 16 * (nq // 2))
                    v.tensor_copy(ctstage[nq % 2][:, 0:qsub], ctp[:, 0:qsub]
                                  ).then_inc(s["ctev"], 1)
                    nq += 1
                v.memset(dummy[0:1, :], 0.0).then_inc(s["slotdone"], 1)

        @block.scalar
        def _(sc):
            ksc = 0
            nmk = 0
            nbt = 0
            for sl, qsub, win, qlist in sched:
                for qs, kl in qlist:
                    if layer == 0:
                        nbt += 1
                        sc.wait_ge(s["bt"], nbt)
                    for m, (ks, D) in enumerate(kl):
                        if D is not None:
                            nmk += 1
                            sc.wait_ge(s["mk"], nmk)
                        else:
                            sc.wait_ge(s["sc"], ksc + 1)
                        if ksc >= 2:
                            sc.wait_ge(s["pb"], ksc - 1)
                        bias = btab[:, ks:ks + 1] if layer == 0 else mcol_sb[:]
                        sc.activation(probs[ksc % 2][:, 0:qsub], scps[ksc % 2][:, 0:qsub],
                                      AF.Exp, bias=bias, scale=SCALE
                                      ).then_inc(s["exp"], 1)
                        ksc += 1

        n_ags = [0]

        @block.gpsimd
        def _(g):
            nq = 0
            last_sl = sched[-1][0]
            for sl, qsub, win, qlist in sched:
                for qs, kl in qlist:
                    g.wait_ge(s["ctev"], nq + 1)
                    cs0, o = qs // 512, qs % 512
                    g.dma_start(ctxT_o[cs0 * 512 + sl * 128:
                                       cs0 * 512 + (sl + 1) * 128, o:o + qsub],
                                ctstage[nq % 2][:, 0:qsub]).then_inc(out_p[nq % 2], 16)
                    nq += 1
                    if sl == last_sl:
                        while (n_ags[0] + 1) * 512 <= qs + qsub:
                            ncs = n_ags[0]
                            g.wait_ge(out_p[0], 16 * ((nq + 1) // 2))
                            g.wait_ge(out_p[1], 16 * (nq // 2))
                            g.collective_compute(
                                "AllGather", ALU.bypass, replica_groups=io["rg"],
                                ins=[ctxT_o[ncs * 512:(ncs + 1) * 512, :]],
                                outs=[io["ctxT_g"][ncs * 4096:(ncs + 1) * 4096, :]]
                                ).then_inc(io["agc_s"], 1)
                            n_ags[0] += 1
    n_ks = sum(len(kl) for _, _, _, ql in sched for _, kl in ql)
    n_mk = sum(1 for _, _, _, ql in sched for _, kl in ql for _, D in kl
               if D is not None)
    n_q = sum(len(ql) for _, _, _, ql in sched)
    n_vch = (2 * n_q if layer == 0 else 0) + 2 * n_q
    tot = {s["const"]: 16 * n_const, s["lds0"]: ptot[0], s["lds1"]: ptot[1],
           s["sc"]: n_ks, s["mk"]: n_mk, s["exp"]: n_ks, s["pb"]: n_ks,
           s["cv"]: n_q, s["ctp"]: n_q, s["ctev"]: n_q,
           s["out0"]: 16 * ((n_q + 1) // 2), s["out1"]: 16 * (n_q // 2),
           s["slotdone"]: len(sched),
           s["ones"]: sum(1 for sl_, _, _, _ in sched
                          if (layer == 0 and sl_ < nbuf) or (layer == 1 and sl_ == 0)),
           io["agc_s"]: n_ags[0],
           s["vch"]: NCH_F[0]}
    if layer == 0:
        tot[s["bt"]] = n_q
    return tot


def emit_out(nc, block, st0, s, layer, pfx, io):
    ctxT = io["ctxT"]
    x_in = io["x_cols"]
    ident_in = io["ident"]
    if layer == 0:
        w_in = io["woutp"]
        b_in = io["b_row"]
        kmat_in = io["kmat"]
        xred_in = io["xredT"]
    else:
        oc_in = io["o_codesT"]
        os_in = io["o_scalesT"]
        ffn_in = io["ffn1"]
    xout = io["xout"]
    CH = 16 if _OUT_TRUNC is None else _OUT_TRUNC

    with ExitStack() as st:
        sb = lambda n, s_, d: st.enter_context(nc.sbuf_tensor(pfx + n, s_, d))
        psum = lambda n, s_, d: st.enter_context(nc.psum_tensor(pfx + n, s_, d))
        ct_p = [s["ct0"], s["ct1"]]
        ld_p = [s["ld0"], s["ld1"]]
        store_p = [s["store0"], s["store1"]]
        W_sb = sb("W_sb", [128, 32 * 512], BF16)
        ident = sb("ident_sb", [128, 128], BF16)
        ones_sb = sb("ones_sb", [1, 128], BF16)
        dummyo = sb("dummyo", [1, 8], F32)
        psO = [psum(f"psO{i}", [128, 512], F32) for i in range(2)]
        tp = [psum(f"tp{i}", [128, 1024], BF16) for i in range(2)]

        if layer == 0:
            ksb8 = sb("ksb8", [128, 4 * 512], I8)
            ksb = sb("ksb", [128, 4 * 512], BF16)
            brow_sb = sb("brow_sb", [1, FC], BF16)

            @block.gpsimd
            def _(g):
                g.dma_start(ident[:], ident_in).then_inc(s["ident"], 16)
                g.dma_start(W_sb[:].rearrange("p (n f) -> p n f", f=512),
                            w_in.rearrange("(n p) f -> p n f", p=128)
                            ).then_inc(s["wl"], 16)
                g.dma_start(brow_sb[:], b_in).then_inc(s["wl"], 16)
                g.dma_start(ksb8[:].rearrange("p (n f) -> p n f", f=512),
                            kmat_in.rearrange("(n p) f -> p n f", p=128)
                            ).then_inc(s["wl"], 16)
            WL_FINAL = 48

            @block.scalar
            def _(sc):
                sc.wait_ge(s["wl"], WL_FINAL)
                sc.activation(ksb[:], ksb8[:], AF.Copy).then_inc(s["prep"], 1)
            PREP_FINAL = 1
            TPC_FINAL = 0
        else:
            with ExitStack() as stp:
                sbp = lambda n, s_, d: stp.enter_context(nc.sbuf_tensor(pfx + n, s_, d))
                oc_sb = [sbp(f"oc{i}", [128, H], I32) for i in range(2)]
                deq_sb = [sbp(f"dq{i}", [128, H], BF16) for i in range(2)]
                osc_sb = sbp("osc_sb", [128, 4 * 32], F32)

                @block.gpsimd
                def _(g):
                    g.dma_start(ident[:], ident_in).then_inc(s["ident"], 16)
                    g.dma_start(osc_sb[:].rearrange("p (n f) -> p n f", f=32),
                                os_in.rearrange("(n p) f -> p n f", p=128)
                                ).then_inc(s["wl"], 16)
                    for ft in range(4):
                        if ft >= 2:
                            g.wait_ge(s["prep"], ft - 1)
                        g.dma_start(oc_sb[ft % 2][:], oc_in[ft * 128:(ft + 1) * 128, :]
                                    ).then_inc(ct_p[ft % 2], 16)
                WL_FINAL = 16

                @block.scalar
                def _(sc):
                    for ft in range(4):
                        sc.wait_ge(ct_p[ft % 2], 16 * (ft // 2 + 1))
                        if ft >= 2:
                            sc.wait_ge(s["tpw"], 32 * (ft - 1))
                        sc.activation(deq_sb[ft % 2][:], oc_sb[ft % 2][:], AF.Copy,
                                      bias=-8.0).then_inc(s["prep"], 1)
                PREP_FINAL = 4

                @block.vector
                def _(v):
                    nt = 0
                    v.wait_ge(s["wl"], 16)
                    for ft in range(4):
                        v.wait_ge(s["prep"], ft + 1)
                        for ht in range(32):
                            ins = v.tensor_scalar(
                                deq_sb[ft % 2][:, ht * 128:(ht + 1) * 128],
                                deq_sb[ft % 2][:, ht * 128:(ht + 1) * 128],
                                osc_sb[:, ft * 32 + ht:ft * 32 + ht + 1], None, ALU.mult)
                        ins.then_inc(s["dq"], 1)
                        for ht in range(32):
                            v.wait_ge(s["tpw"], nt + 1)
                            v.tensor_copy(
                                W_sb[:, ht * 512 + ft * 128:ht * 512 + (ft + 1) * 128],
                                tp[nt % 2][:, 0:128]).then_inc(s["tpc"], 1)
                            nt += 1

                @block.tensor
                def _(t):
                    t.wait_ge(s["ident"], 16)
                    nt = 0
                    for ft in range(4):
                        t.wait_ge(s["dq"], ft + 1)
                        for ht in range(32):
                            if nt >= 2:
                                t.wait_ge(s["tpc"], nt - 1)
                            t.transpose(tp[nt % 2][:, 0:128],
                                        deq_sb[ft % 2][:, ht * 128:(ht + 1) * 128],
                                        ident[:]).then_inc(s["tpw"], 1)
                            nt += 1
            TPC_FINAL = 128

        ctx_sb = [sb(f"cx{i}", [128, 32 * 512], BF16) for i in range(2)]
        x_sb = [sb(f"x{i}", [128, 4 * 512], F32) for i in range(2)]
        out_sb = [sb(f"o{i}", [128, 4 * 512], F32) for i in range(2)]
        if layer == 0:
            xr_sb = [sb(f"xr{i}", [128, 4 * 512], BF16) for i in range(2)]
        else:
            ffn_sb = [sb(f"fn{i}", [128, 4 * 512], BF16) for i in range(2)]

        @block.vector
        def _(v):
            v.memset(ones_sb[:], 1.0).then_inc(s["ones"], 1)

        def gates(e):
            e.wait_ge(s["wl"], WL_FINAL)
            e.wait_ge(s["ident"], 16)
            e.wait_ge(s["prep"], PREP_FINAL)
            e.wait_ge(s["ones"], 1)
            if layer == 1:
                e.wait_ge(s["tpc"], TPC_FINAL)

        N_LD = 3

        @block.sync
        def _(sync):
            gates(sync)
            for cs in range(CH):
                if cs >= 2:
                    sync.wait_ge(s["chunkdone"], cs - 1)
                if "ctx_gate" in io:
                    sync.wait_ge(io["ctx_gate"], cs + 1)
                sync.dma_start(ctx_sb[cs % 2][:].rearrange("p (n f) -> p n f", f=512),
                               ctxT[cs * 4096:(cs + 1) * 4096, :].rearrange(
                                   "(n p) f -> p n f", p=128)).then_inc(ld_p[cs % 2], 16)
                sync.dma_start(x_sb[cs % 2][:].rearrange("p (n f) -> p n f", f=512),
                               x_in[cs * 512:(cs + 1) * 512, :].rearrange(
                                   "(j p) f -> p j f", p=128)).then_inc(ld_p[cs % 2], 16)
                if layer == 0:
                    sync.dma_start(xr_sb[cs % 2][:].rearrange("p (n f) -> p n f", f=512),
                                   xred_in[:, cs * 512:(cs + 1) * 512].rearrange(
                                       "(n p) f -> p n f", p=128)).then_inc(ld_p[cs % 2], 16)
                else:
                    sync.dma_start(ffn_sb[cs % 2][:].rearrange("p (n f) -> p n f", f=512),
                                   ffn_in[cs * 512:(cs + 1) * 512, :].rearrange(
                                       "(j p) f -> p j f", p=128)).then_inc(ld_p[cs % 2], 16)

        @block.tensor
        def _(t):
            gates(t)
            for cs in range(CH):
                t.wait_ge(ld_p[cs % 2], N_LD * 16 * (cs // 2 + 1))
                for stx in range(4):
                    i = 4 * cs + stx
                    par = i % 2
                    if i >= 2:
                        t.wait_ge(s["ev"], i - 1)
                    for ht in range(32):
                        ins = t.matmul(psO[par][:],
                                       ctx_sb[cs % 2][:, ht * 512 + stx * 128:
                                                      ht * 512 + stx * 128 + 128],
                                       W_sb[:, ht * 512:(ht + 1) * 512],
                                       start=(ht == 0), stop=(layer == 1 and ht == 31))
                    if layer == 0:
                        for rt in range(4):
                            t.matmul(psO[par][:],
                                     xr_sb[cs % 2][:, rt * 512 + stx * 128:
                                                   rt * 512 + stx * 128 + 128],
                                     ksb[:, rt * 512:(rt + 1) * 512],
                                     start=False, stop=False)
                        t.matmul(psO[par][:], ones_sb[:], brow_sb[:],
                                 start=False, stop=True).then_inc(s["mm"], 1)
                    else:
                        ins.then_inc(s["mm"], 1)

        nch_f = [0]

        @block.vector
        def _(v):
            nch = nch_f
            for cs in range(CH):
                for stx in range(4):
                    i = 4 * cs + stx
                    par = i % 2
                    v.wait_ge(s["mm"], i + 1)
                    if cs >= 2 and stx == 0:
                        v.wait_ge(store_p[cs % 2], 16 * (cs // 2))
                    ins = v.tensor_tensor(out_sb[cs % 2][:, stx * 512:(stx + 1) * 512],
                                          psO[par][:],
                                          x_sb[cs % 2][:, stx * 512:(stx + 1) * 512],
                                          ALU.add)
                    if layer == 1:
                        nch[0] += 1
                        ins.then_inc(s["vch"], 1)
                        v.wait_ge(s["vch"], nch[0])
                        v.tensor_tensor(out_sb[cs % 2][:, stx * 512:(stx + 1) * 512],
                                        out_sb[cs % 2][:, stx * 512:(stx + 1) * 512],
                                        ffn_sb[cs % 2][:, stx * 512:(stx + 1) * 512],
                                        ALU.add).then_inc(s["ev"], 1)
                    else:
                        ins.then_inc(s["ev"], 1)
                v.memset(dummyo[0:1, :], 0.0).then_inc(s["chunkdone"], 1)

        @block.gpsimd
        def _(g):
            for cs in range(CH):
                g.wait_ge(s["ev"], 4 * (cs + 1))
                g.dma_start(xout[cs * 512:(cs + 1) * 512, :].rearrange(
                    "(j p) f -> p j f", p=128),
                    out_sb[cs % 2][:].rearrange("p (j f) -> p j f", f=512)
                ).then_inc(store_p[cs % 2], 16)
            g.wait_ge(store_p[0], 16 * ((CH + 1) // 2))
            g.wait_ge(store_p[1], 16 * (CH // 2))
    tot = {s["wl"]: WL_FINAL, s["ident"]: 16, s["prep"]: PREP_FINAL, s["ones"]: 1,
           s["mm"]: CH * 4, s["ev"]: CH * 4, s["chunkdone"]: CH, s["vch"]: nch_f[0],
           s["ld0"]: 3 * 16 * ((CH + 1) // 2), s["ld1"]: 3 * 16 * (CH // 2),
           s["store0"]: 16 * ((CH + 1) // 2), s["store1"]: 16 * (CH // 2)}
    if layer == 1:
        tot[s["ct0"]] = 32
        tot[s["ct1"]] = 32
        tot[s["dq"]] = 4
        tot[s["tpw"]] = 128
        tot[s["tpc"]] = 128
    return tot




def emit_lnphase(nc, block, st0, s, pfx, io, rg):
    x_ap = io["x_cols"]
    eps_in = io["epscol"]
    ident_in = io["ident"]
    arin = io["arin"]
    arout = io["arout"]
    agin = io["agin"]
    agout = io["agout"]
    T = 64
    with ExitStack() as st:
        sb = lambda n, s_, d: st.enter_context(nc.sbuf_tensor(pfx + n, s_, d))
        psum = lambda n, s_, d: st.enter_context(nc.psum_tensor(pfx + n, s_, d))
        xb = [sb(f"xb{i}", [128, 512], F32) for i in range(2)]
        sq = [sb(f"sq{i}", [128, 512], F32) for i in range(2)]
        stat = sb("stat", [128, 128], F32)
        statg = sb("statg", [128, 128], F32)
        meanv = sb("meanv", [128, 64], F32)
        ex2 = sb("ex2", [128, 64], F32)
        negm2 = sb("negm2", [128, 64], F32)
        varv = sb("varv", [128, 64], F32)
        lnv = sb("lnv", [128, 64], F32)
        rstd = sb("rstd", [128, 64], F32)
        negmr = sb("negmr", [128, 64], F32)
        eps_sb = sb("eps", [128, 1], F32)
        ident = sb("ident", [128, 128], BF16)
        xnb = [sb(f"xnb{i}", [128, 512], BF16) for i in range(2)]
        xstage = sb("xstage", [128, 4 * 512], BF16)
        dummy = sb("dummy", [1, 8], F32)
        tp = [psum(f"tp{i}", [128, 1024], BF16) for i in range(2)]
        x_p = [s["x0"], s["x1"]]
        x2_p = [s["x20"], s["x21"]]

        @block.sync
        def _(sync):
            sync.dma_start(eps_sb[:], eps_in).then_inc(s["cst"], 16)
            sync.dma_start(ident[:], ident_in).then_inc(s["cst"], 16)
            for t in range(T):
                if t >= 2:
                    sync.wait_ge(s["sq"], t - 1)
                    sync.wait_ge(s["red"], t - 1)
                sync.dma_start(xb[t % 2][:], x_ap[t * 128:(t + 1) * 128, :]
                               ).then_inc(x_p[t % 2], 16)
            sync.wait_ge(s["stats"], 1)
            for t in range(T):
                if t >= 2:
                    sync.wait_ge(s["xn"], t - 1)
                sync.dma_start(xb[t % 2][:], x_ap[t * 128:(t + 1) * 128, :]
                               ).then_inc(x2_p[t % 2], 16)

        @block.vector
        def _(v):
            nch = [0]
            def edge(ins):
                nch[0] += 1
                ins.then_inc(s["vch"], 1)
                v.wait_ge(s["vch"], nch[0])
            for t in range(T):
                v.wait_ge(x_p[t % 2], 16 * (t // 2 + 1))
                v.reduce_sum(stat[:, t:t + 1], xb[t % 2][:], axis=AX.X
                             ).then_inc(s["red"], 1)
            v.wait_ge(s["stats"], 1)
            edge(v.tensor_copy(statg[:], stat[:]) if False else
                 v.tensor_scalar(meanv[:], statg[:, 0:64], 1.0 / H, None, ALU.mult))
            edge(v.tensor_scalar(ex2[:], statg[:, 64:128], 1.0 / H, None, ALU.mult))
            edge(v.scalar_tensor_tensor(negm2[:], meanv[:], -1.0, meanv[:],
                                        ALU.mult, ALU.mult))
            edge(v.tensor_tensor(varv[:], ex2[:], negm2[:], ALU.add))
            v.wait_ge(s["cst"], 32)
            v.tensor_scalar(varv[:], varv[:], eps_sb[:], None, ALU.add
                            ).then_inc(s["var"], 1)
            v.wait_ge(s["sqrt"], 1)
            edge(v.tensor_copy(rstd[:], lnv[:]))
            v.scalar_tensor_tensor(negmr[:], meanv[:], -1.0, rstd[:],
                                   ALU.mult, ALU.mult).then_inc(s["rn"], 1)
            v.wait_ge(s["rn"], 1)
            for t in range(T):
                v.wait_ge(x2_p[t % 2], 16 * (t // 2 + 1))
                v.tensor_scalar(xnb[t % 2][:], xb[t % 2][:], rstd[:, t:t + 1],
                                negmr[:, t:t + 1], ALU.mult, ALU.add
                                ).then_inc(s["xn"], 1)
                if t >= 1:
                    j = t - 1
                    v.wait_ge(s["tp"], j + 1)
                    if j % 4 == 0 and j >= 4:
                        v.wait_ge(s["ag"], 16 * 4 * (j // 4))
                    for ft in range(4):
                        ins = v.tensor_copy(
                            xstage[:, ft * 512 + (j % 4) * 128:
                                   ft * 512 + (j % 4) * 128 + 128],
                            tp[j % 2][:, ft * 128:(ft + 1) * 128])
                    ins.then_inc(s["tpev"], 1)
            j = T - 1
            v.wait_ge(s["tp"], T)
            for ft in range(4):
                ins = v.tensor_copy(xstage[:, ft * 512 + (j % 4) * 128:
                                           ft * 512 + (j % 4) * 128 + 128],
                                    tp[j % 2][:, ft * 128:(ft + 1) * 128])
            ins.then_inc(s["tpev"], 1)
            NCHL[0] = nch[0]

        @block.scalar
        def _(sc):
            for t in range(T):
                sc.wait_ge(x_p[t % 2], 16 * (t // 2 + 1))
                sc.activation(sq[t % 2][:], xb[t % 2][:], AF.Square,
                              accum_out=stat[:, 64 + t:64 + t + 1]).then_inc(s["sq"], 1)
            sc.wait_ge(s["var"], 1)
            sc.activation(lnv[:], varv[:], AF.Ln).then_inc(s["ln"], 1)
            sc.wait_ge(s["ln"], 1)
            sc.activation(lnv[:], lnv[:], AF.Exp, scale=-0.5).then_inc(s["sqrt"], 1)

        @block.tensor
        def _(t_):
            t_.wait_ge(s["cst"], 32)
            for t in range(T):
                t_.wait_ge(s["xn"], t + 1)
                if t >= 2:
                    t_.wait_ge(s["tpev"], t - 1)
                for ft in range(4):
                    ins = t_.transpose(tp[t % 2][:, ft * 128:(ft + 1) * 128],
                                       xnb[t % 2][:, ft * 128:(ft + 1) * 128], ident[:])
                ins.then_inc(s["tp"], 1)

        @block.gpsimd
        def _(g):
            g.wait_ge(s["red"], 64)
            g.wait_ge(s["sq"], 64)
            g.dma_start(arin, stat[:]).then_inc(s["arx"], 16)
            g.wait_ge(s["arx"], 16)
            g.collective_compute("AllReduce", ALU.add, replica_groups=rg,
                                 ins=[arin], outs=[arout]).then_inc(s["arc"], 1)
            g.wait_ge(s["arc"], 1)
            g.dma_start(statg[:], arout).then_inc(s["arx"], 16)
            g.wait_ge(s["arx"], 32)
            g.sem_inc(s["stats"], 1)
            for grp in range(16):
                g.wait_ge(s["tpev"], 4 * (grp + 1))
                for ft in range(4):
                    g.dma_start(agin[grp * 512 + ft * 128:grp * 512 + (ft + 1) * 128, :],
                                xstage[:, ft * 512:(ft + 1) * 512]).then_inc(s["ag"], 16)
                g.wait_ge(s["ag"], 16 * 4 * (grp + 1))
                g.collective_compute("AllGather", ALU.bypass, replica_groups=rg,
                                     ins=[agin[grp * 512:(grp + 1) * 512, :]],
                                     outs=[agout[grp * 4096:(grp + 1) * 4096, :]]
                                     ).then_inc(s["agc"], 1)
    tot = {s["cst"]: 32, s["x0"]: 16 * 32, s["x1"]: 16 * 32, s["x20"]: 16 * 32,
           s["x21"]: 16 * 32, s["sq"]: 64, s["red"]: 64, s["var"]: 1, s["ln"]: 1,
           s["sqrt"]: 1, s["rn"]: 1, s["xn"]: 64, s["tp"]: 64, s["tpev"]: 64,
           s["ag"]: 16 * 64, s["agc"]: 16, s["arx"]: 32, s["arc"]: 1, s["stats"]: 1,
           s["vch"]: NCHL[0]}
    return tot


NCHL = [0]

LN_SEMS = ["cst", "x0", "x1", "x20", "x21", "sq", "red", "var", "ln", "sqrt", "rn",
           "xn", "tp", "tpev", "ag", "agc", "arx", "arc", "stats", "vch",
           "agc1", "arc1"]
PROJ_SEMS = ["wl", "ident", "ct0", "ct1", "prep", "dq", "tpw", "tpc", "xnld0",
             "xnld1", "mm", "mm1", "t1ev", "tt", "t1T", "qn", "evt", "tp", "tpev",
             "store", "chunkdone", "sel", "vch"]
ATTN_SEMS = ["const", "lds0", "lds1", "sc", "mk", "bt", "exp", "pb", "cv", "ctp",
             "ctev", "out0", "out1", "slotdone", "ones", "vch", "agc", "agc1"]
OUT_SEMS = ["wl", "ident", "ct0", "ct1", "prep", "dq", "tpw", "tpc", "ld0", "ld1",
            "mm", "ev", "store0", "store1", "chunkdone", "ones", "vch"]


def build_fused():
    nc = bass.Bass("TRN2", target_bir_lowering=False)
    rg = [list(range(NC))]
    ap = lambda *a, **k: nc.dram_tensor(*a, **k).ap()
    io = {}
    io["ident"] = ap("ident", [128, 128], BF16, kind="ExternalInput")
    io["epscol"] = ap("epscol", [128, 1], F32, kind="ExternalInput")
    io["x_cols0"] = ap("x_cols0", [S, FC], F32, kind="ExternalInput")
    io["w_all"] = ap("w_all", [H, 1536], F32, kind="ExternalInput")
    io["b_row0"] = ap("b_row0", [1, 1536], F32, kind="ExternalInput")
    for q in (64, 256, 512):
        nD = {64: 2, 256: 2, 512: 4}[q]
        io[f"mask{q}"] = ap(f"mask{q}", [128, nD * q], F32, kind="ExternalInput")
    io["slope_bc"] = ap("slope_bc", [128, 4], F32, kind="ExternalInput")
    io["kpos"] = ap("kpos", [128, 64], F32, kind="ExternalInput")
    io["mcol"] = ap("mcol", [128, 1], F32, kind="ExternalInput")
    io["woutp"] = ap("woutp", [H, FC], F32, kind="ExternalInput")
    io["bout_row"] = ap("bout_row", [1, FC], F32, kind="ExternalInput")
    io["kmat"] = ap("kmat", [512, FC], I8, kind="ExternalInput")
    io["codesT_all"] = ap("codesT_all", [768, H], I32, kind="ExternalInput")
    io["scalesT_all"] = ap("scalesT_all", [768, 32], F32, kind="ExternalInput")
    io["wffnT"] = ap("wffnT", [H, FC], F32, kind="ExternalInput")
    io["bffn_row"] = ap("bffn_row", [1, FC], F32, kind="ExternalInput")
    io["loraA_arr"] = ap("loraA_arr", [128, 32 * 48], F32, kind="ExternalInput")
    io["loraB_arr"] = ap("loraB_arr", [16, 3 * FC], F32, kind="ExternalInput")
    io["lsi"] = ap("lsi", [1, 4], I32, kind="ExternalInput")
    io["cos_arr"] = ap("cos_arr", [128, 4096], F32, kind="ExternalInput")
    io["sin_arr"] = ap("sin_arr", [128, 4096], F32, kind="ExternalInput")
    io["o_codesT"] = ap("o_codesT", [FC, H], I32, kind="ExternalInput")
    io["o_scalesT"] = ap("o_scalesT", [FC, 32], F32, kind="ExternalInput")
    io["xout"] = ap("xout", [S, FC], F32, kind="ExternalOutput")
    for l in range(2):
        io[f"arin{l}"] = ap(f"arin{l}", [128, 128], F32, kind="Internal")
        io[f"arout{l}"] = ap(f"arout{l}", [128, 128], F32, kind="Internal")
        io[f"agin{l}"] = ap(f"agin{l}", [S, 512], BF16, kind="Internal")
        io[f"xnT{l}"] = ap(f"xnT{l}", [16 * H, 512], BF16, kind="Internal")
        io[f"ctgin{l}"] = ap(f"ctgin{l}", [S, 512], BF16, kind="Internal")
        io[f"ctxT{l}"] = ap(f"ctxT{l}", [16 * H, 512], BF16, kind="Internal")
        io[f"qT{l}"] = ap(f"qT{l}", [512, S], BF16, kind="Internal")
        io[f"kT{l}"] = ap(f"kT{l}", [(512 if l == 0 else 128), S], BF16, kind="Internal")
        io[f"v4{l}"] = ap(f"v4{l}", [(4 if l == 0 else 1), 64, 128, 128], BF16,
                          kind="Internal")
    io["xredT"] = ap("xredT", [512, S], BF16, kind="Internal")
    io["ffn1"] = ap("ffn1", [S, FC], BF16, kind="Internal")
    io["x1"] = ap("x1", [S, FC], F32, kind="Internal")

    with ExitStack() as st0:
        sems = {}
        for kind, names in (("ln", LN_SEMS), ("proj", PROJ_SEMS),
                            ("attn", ATTN_SEMS), ("out", OUT_SEMS)):
            sems[kind] = {n: st0.enter_context(nc.semaphore(f"{kind}_{n}"))
                          for n in names}
        PH = st0.enter_context(nc.semaphore("PH"))
        ph = [0]
        block = st0.enter_context(nc.Block())
        tr = _Track()
        wblock = _BlkW(block, tr)
        exp_tot = {}

        def barrier(tot):
            for sem, val in tot.items():
                exp_tot[sem] = exp_tot.get(sem, 0) + val
            @block.vector
            def _(v):
                for sem, val in tot.items():
                    v.wait_ge(sem, val + tr.base.get(sem, 0))
                v.memset(nc.sbuf_barrier_dummy[0:1, :], 0.0).then_inc(PH, 1)
            @block.sync
            def _(e):
                e.sem_inc(PH, 1)
            @block.scalar
            def _(e):
                e.sem_inc(PH, 1)
            @block.tensor
            def _(e):
                e.sem_inc(PH, 1)
            @block.gpsimd
            def _(e):
                e.sem_inc(PH, 1)
            ph[0] += 5
            tgt = ph[0]
            @block.sync
            def _(e):
                e.wait_ge(PH, tgt)
            @block.scalar
            def _(e):
                e.wait_ge(PH, tgt)
            @block.tensor
            def _(e):
                e.wait_ge(PH, tgt)
            @block.vector
            def _(e):
                e.wait_ge(PH, tgt)
            @block.gpsimd
            def _(e):
                e.wait_ge(PH, tgt)

        nphase = [0]
        with nc.sbuf_tensor("barrier_dummy", [1, 8], F32) as bd:
            nc.sbuf_barrier_dummy = bd
            for l in range(2):
                if nphase[0] >= _FUSE_PHASES:
                    break
                sfx = str(l)
                x_ap = io["x_cols0"] if l == 0 else io["x1"]
                ln_s = dict(sems["ln"])
                if l == 1:
                    ln_s["arc"] = sems["ln"]["arc1"]
                    ln_s["agc"] = sems["ln"]["agc1"]
                tot = emit_lnphase(nc, wblock, st0, ln_s, f"L{l}ln_",
                                   {"x_cols": x_ap, "epscol": io["epscol"],
                                    "ident": io["ident"], "arin": io[f"arin{l}"],
                                    "arout": io[f"arout{l}"], "agin": io[f"agin{l}"],
                                    "agout": io[f"xnT{l}"]}, rg)
                tot = dict(tot)
                ag_tot = tot.pop(ln_s["agc"])
                barrier(tot)
                nphase[0] += 1
                if nphase[0] >= _FUSE_PHASES:
                    break
                pio = {"xnT": io[f"xnT{l}"], "ident": io["ident"],
                       "xn_gate": ln_s["agc"],
                       "qT": io[f"qT{l}"], "kT": io[f"kT{l}"], "v4": io[f"v4{l}"]}
                if l == 0:
                    pio.update({"w_all": io["w_all"], "b_row": io["b_row0"],
                                "xredT": io["xredT"]})
                else:
                    pio.update({k: io[k] for k in
                                ("codesT_all", "scalesT_all", "wffnT", "bffn_row",
                                 "loraA_arr", "loraB_arr", "lsi", "cos_arr",
                                 "sin_arr", "ffn1")})
                tot = emit_proj(nc, wblock, st0, sems["proj"], l, f"L{l}pj_", pio)
                tot = dict(tot)
                tot[ln_s["agc"]] = ag_tot
                barrier(tot)
                nphase[0] += 1
                if nphase[0] >= _FUSE_PHASES:
                    break
                agc_s = sems["attn"]["agc" if l == 0 else "agc1"]
                aio = {"qT": io[f"qT{l}"], "kT": io[f"kT{l}"], "v4": io[f"v4{l}"],
                       "ident": io["ident"], "ctxT": io[f"ctgin{l}"],
                       "rg": rg, "ctxT_g": io[f"ctxT{l}"], "agc_s": agc_s,
                       "mask64": io["mask64"], "mask256": io["mask256"],
                       "mask512": io["mask512"]}
                if l == 0:
                    aio.update({"slope_bc": io["slope_bc"], "kpos": io["kpos"]})
                else:
                    aio.update({"mcol": io["mcol"]})
                tot = emit_attn(nc, wblock, st0, sems["attn"], l, f"L{l}at_", aio)
                tot = dict(tot)
                cag_tot = tot.pop(agc_s)
                barrier(tot)
                nphase[0] += 1
                if nphase[0] >= _FUSE_PHASES:
                    break
                oio = {"ctxT": io[f"ctxT{l}"], "x_cols": x_ap, "ident": io["ident"],
                       "ctx_gate": agc_s,
                       "xout": io["x1"] if l == 0 else io["xout"]}
                if l == 0:
                    oio.update({"woutp": io["woutp"], "b_row": io["bout_row"],
                                "kmat": io["kmat"], "xredT": io["xredT"]})
                else:
                    oio.update({"o_codesT": io["o_codesT"],
                                "o_scalesT": io["o_scalesT"], "ffn1": io["ffn1"]})
                tot = emit_out(nc, wblock, st0, sems["out"], l, f"L{l}ot_", oio)
                tot = dict(tot)
                tot[agc_s] = cag_tot
                barrier(tot)
                nphase[0] += 1
                if l == 0:
                    mism = {k: (exp_tot.get(k, 0), tr.cnt.get(k, 0))
                            for k in set(exp_tot) | set(tr.cnt)
                            if exp_tot.get(k, 0) != tr.cnt.get(k, 0)}
                    assert not mism, f"sem total mismatch after L0: {mism}"
                    tr.base = dict(tr.cnt)
    return nc



_CACHE = {}


def _get(name, fn):
    if name not in _CACHE:
        _CACHE[name] = fn()
    return _CACHE[name]


def _run(nc, in_maps):
    return run_bass_kernel_spmd(nc, in_maps, core_ids=list(range(NC)))


def _masks_np(qsub):
    Ds = {64: [64, 0], 256: [0, -128], 512: [0, -128, -256, -384]}[qsub]
    m = np.zeros((128, len(Ds) * qsub), np.float32)
    for vi, D in enumerate(Ds):
        i = np.arange(128)[:, None]
        j = np.arange(qsub)[None, :]
        m[:, vi * qsub:(vi + 1) * qsub] = np.where(i > j + D, NEG, 0.0)
    return m


def _parity_perm():
    return np.concatenate([np.arange(0, 128, 2), np.arange(1, 128, 2)])


def kernel(**inp):
    f32 = np.float32
    x0 = np.ascontiguousarray(np.asarray(inp["L0_attn_hidden_states"],
                                         f32).reshape(S, H))
    ident = np.eye(128, dtype=BF)
    slopes = _alibi_slopes(32)
    kpos = (np.arange(128)[:, None] + 128.0 * np.arange(64)[None, :]).astype(f32)
    msk = {q: _masks_np(q) for q in (64, 256, 512)}
    epscol = np.full((128, 1), 1e-5, f32)
    mcol = np.full((128, 1), -M_STAB, f32)

    wqkv = np.asarray(inp["L0_attn_w_qkv"], f32)
    bqkv = np.asarray(inp["L0_attn_b_qkv"], f32)
    wout = np.asarray(inp["L0_attn_w_out"], f32)
    bout = np.asarray(inp["L0_attn_b_out"], f32)
    kern = np.ascontiguousarray(np.asarray(inp["L0_ffn_kernel"]).reshape(512, H))
    perm = np.zeros(32, np.int64)
    for cc in range(NC):
        for j in range(4):
            perm[4 * cc + j] = cc + 8 * j
    pperm = _parity_perm()
    qc = np.asarray(inp["L1_attn_q_codes"])
    qs_ = np.asarray(inp["L1_attn_q_scales"], f32)
    kc_ = np.asarray(inp["L1_attn_k_codes"])
    ks_ = np.asarray(inp["L1_attn_k_scales"], f32)
    vc = np.asarray(inp["L1_attn_v_codes"])
    vs_ = np.asarray(inp["L1_attn_v_scales"], f32)
    wffn = np.asarray(inp["L1_ffn_weight"], f32)
    bffn = np.asarray(inp["L1_ffn_bias"], f32)
    lA = np.asarray(inp["L1_ffn_lora_A"], f32)
    lB = np.asarray(inp["L1_ffn_lora_B"], f32)
    lsi = np.ascontiguousarray(
        np.asarray(inp["L1_ffn_lora_start_indices"]).astype(np.int32).reshape(1, 4))
    oc = np.asarray(inp["L1_attn_o_codes"])
    osc = np.asarray(inp["L1_attn_o_scales"], f32)
    pos = np.arange(S, dtype=np.float64)
    invf = 1.0 / (10000.0 ** (np.arange(0, HD, 2, dtype=np.float64) / HD))
    ang = pos[:, None] * invf[None, :]
    cos_arr = np.ascontiguousarray(np.cos(ang).astype(f32).reshape(64, 128, 64)
                                   .transpose(1, 0, 2).reshape(128, 4096))
    sin_arr = np.ascontiguousarray(np.sin(ang).astype(f32).reshape(64, 128, 64)
                                   .transpose(1, 0, 2).reshape(128, 4096))
    lAarr = np.ascontiguousarray(
        lA.transpose(2, 0, 1).reshape(32, 128, 3, 16).transpose(1, 0, 2, 3)
        .reshape(128, 32 * 48))

    in_maps = []
    for c in range(NC):
        heads = [c, c + 8, c + 16, c + 24]
        qcols = np.concatenate([np.arange(h * 128, (h + 1) * 128) for h in heads])
        w_all = np.concatenate([wqkv[:, qcols], wqkv[:, H + qcols],
                                wqkv[:, 2 * H + qcols]], axis=1)
        b_row0 = np.concatenate([bqkv[qcols], bqkv[H + qcols],
                                 bqkv[2 * H + qcols]])[None, :]
        slope_bc = np.ascontiguousarray(
            np.broadcast_to(slopes[heads][None, :], (128, 4)).astype(f32))
        cols = np.arange(c * FC, (c + 1) * FC)
        woutp = np.concatenate([wout[p * 128:(p + 1) * 128, cols] for p in perm],
                               axis=0)
        qcols1 = np.concatenate([(4 * c + hh) * 128 + pperm for hh in range(4)])
        kcols1 = c * 128 + pperm
        vcols1 = np.arange(c * 128, (c + 1) * 128)
        codesT = np.ascontiguousarray(np.concatenate(
            [qc[:, qcols1].T, kc_[:, kcols1].T, vc[:, vcols1].T],
            axis=0)).astype(np.int32)
        scalesT = np.ascontiguousarray(np.concatenate(
            [qs_[:, qcols1].T, ks_[:, kcols1].T, vs_[:, vcols1].T],
            axis=0)).astype(f32)
        wffnT = np.ascontiguousarray(wffn[cols, :].T)
        lBarr = np.ascontiguousarray(
            lB[:, cols, :].transpose(0, 2, 1).transpose(1, 0, 2).reshape(16, 3 * FC))
        in_maps.append({
            "ident": ident, "epscol": epscol,
            "x_cols0": np.ascontiguousarray(x0[:, cols]),
            "w_all": np.ascontiguousarray(w_all),
            "b_row0": np.ascontiguousarray(b_row0),
            "mask64": msk[64], "mask256": msk[256], "mask512": msk[512],
            "slope_bc": slope_bc, "kpos": kpos, "mcol": mcol,
            "woutp": np.ascontiguousarray(woutp),
            "bout_row": np.ascontiguousarray(bout[cols][None, :]),
            "kmat": np.ascontiguousarray(kern[:, cols]),
            "codesT_all": codesT, "scalesT_all": scalesT, "wffnT": wffnT,
            "bffn_row": np.ascontiguousarray(bffn[cols][None, :]),
            "loraA_arr": lAarr, "loraB_arr": lBarr, "lsi": lsi,
            "cos_arr": cos_arr, "sin_arr": sin_arr,
            "o_codesT": np.ascontiguousarray(oc[:, cols].T).astype(np.int32),
            "o_scalesT": np.ascontiguousarray(osc[:, cols].T).astype(f32),
        })
    nc = _get("fused", build_fused)
    res = _run(nc, in_maps)
    x2 = np.concatenate([res.results[c]["xout"] for c in range(NC)], axis=1)
    return np.ascontiguousarray(x2.reshape(1, S, H)).astype(np.float32)
